# version 18
# baseline (speedup 1.0000x reference)
import math
import contextlib
import numpy as np

import concourse.bass as bass
import concourse.mybir as mybir
from concourse.bass_utils import run_bass_kernel_spmd

F32 = mybir.dt.float32
BF16 = mybir.dt.bfloat16
U32 = mybir.dt.uint32
I32 = mybir.dt.int32
AF = mybir.ActivationFunctionType
ALU = mybir.AluOpType
AX = mybir.AxisListType

CORES = 8

CFG_FULL = dict(D=2048, S=4096, TOK=1024, DV=16, KSEL=32,
                E=16, TOPK=6, I=1408, ISH=2816, dbg=())
CFG_TINY = dict(D=512, S=256, TOK=64, DV=16, KSEL=32,
                E=8, TOPK=3, I=256, ISH=512, dbg=())


def _ceil(a, b):
    return (a + b - 1) // b


class Builder:

    ENG = ("pe", "dve", "act", "pool", "sp")

    def __init__(self, nc, es):
        self.nc = nc
        self.block = es.enter_context(nc.Block())
        self.es = es
        self.sem = {}
        self.cnt = {}
        self.vc = {e: {} for e in self.ENG}
        self.snap = {}
        for e in self.ENG:
            self.sem[e] = es.enter_context(nc.semaphore(f"sem_{e}"))
            self.cnt[e] = 0
            self.snap[e] = {}
        self.last_w = {}
        self.readers = {}
        self._engmap = dict(pe="tensor", dve="vector", act="scalar",
                            pool="gpsimd", sp="sync")
        self.nwaits = 0
        self.nops = 0
        self.pending = {e: [] for e in self.ENG}

    def new_dma_sem(self, name, nslots=4):
        sns = []
        for i in range(nslots):
            sn = f"dq_{name}{i}"
            self.sem[sn] = self.es.enter_context(self.nc.semaphore(sn))
            self.cnt[sn] = 0
            self.snap[sn] = {}
            sns.append(sn)
        self.streams = getattr(self, "streams", {})
        self.streams[name] = [sns, 0]
        return name

    FULL = (0, 1 << 62, 0, 1 << 62)

    @classmethod
    def _box(cls, a):
        if a is None:
            return None
        if isinstance(a, str):
            return (a, cls.FULL)
        ap = getattr(a, "ap", None)
        t = getattr(a, "tensor", None)
        if t is None:
            return (a.name, cls.FULL)
        name = t.name
        try:
            import concourse.bass_types as _bt
            sbufish = a.space != _bt.MemorySpace.DRAM
        except Exception:
            sbufish = True
        dims = list(ap)
        off = a.offset
        if not sbufish or not dims:
            span = sum(max(0, s) * (c - 1) for s, c in dims) + 1
            return (name, (0, 1, off, off + span))
        row = 1
        for d in t.shape[1:]:
            row *= d
        p0 = off // row if row else 0
        f0 = off % row if row else 0
        pcnt = dims[0][1]
        fspan = sum(max(0, s) * (c - 1) for s, c in dims[1:]) + 1
        return (name, (p0, p0 + pcnt, f0, f0 + fspan))

    @staticmethod
    def _ovl(b1, b2):
        return b1[0] < b2[1] and b2[0] < b1[1] and b1[2] < b2[3] and b2[2] < b1[3]

    def _collect(self, r, w):
        waits = []
        rb = [self._box(a) for a in r if a is not None]
        wb = [self._box(a) for a in w if a is not None]
        for n, bx in rb:
            for obx, sem, cnt in self.last_w.get(n, ()):
                if self._ovl(bx, obx):
                    waits.append((sem, cnt))
        for n, bx in wb:
            for obx, sem, cnt in self.last_w.get(n, ()):
                if self._ovl(bx, obx):
                    waits.append((sem, cnt))
            for obx, sem, cnt in self.readers.get(n, ()):
                if self._ovl(bx, obx):
                    waits.append((sem, cnt))
        return rb, wb, waits

    def _wait_fns(self, waits, eng_name):
        fns = []
        myvc = self.vc[eng_name]
        for semname, count in waits:
            if semname.startswith("dq_"):
                count = self.cnt[semname]
            if count <= 0 or myvc.get(semname, 0) >= count:
                continue
            s = self.sem[semname]
            fns.append((s, count))
            snapd = self.snap[semname]
            best = snapd.get(count)
            if best is None:
                ks = [k for k in snapd if k >= count]
                best = snapd[min(ks)] if ks else (snapd[max(snapd)] if snapd else None)
            if best:
                for k, v in best.items():
                    if k.startswith("dq_"):
                        continue
                    if myvc.get(k, 0) < v:
                        myvc[k] = v
            if myvc.get(semname, 0) < count:
                myvc[semname] = count
        return fns

    def op(self, eng_name, fn, r=(), w=()):
        rn, wn, waits = self._collect(r, w)
        if eng_name == "pe":
            waits = [wt for wt in waits if wt[0] != "pe"]
        fns = self._wait_fns(waits, eng_name)
        sem = self.sem[eng_name]
        newcnt = self.cnt[eng_name] + 1
        self.cnt[eng_name] = newcnt
        self.nwaits += len(fns)
        self.nops += 1

        def body(eng):
            for s, c in fns:
                eng.wait_ge(s, c)
            fn(eng).then_inc(sem, 1)

        self.pending[eng_name].append(body)
        self.snap[eng_name][newcnt] = dict(self.vc[eng_name], **{eng_name: newcnt})
        self._record(rn, wn, eng_name, newcnt)

    @staticmethod
    def _cover(bx, obx):
        return (bx[0] <= obx[0] and obx[1] <= bx[1]
                and bx[2] <= obx[2] and obx[3] <= bx[3])

    def _record(self, rb, wb, semname, newcnt):
        for n, bx in rb:
            self.readers.setdefault(n, []).append((bx, semname, newcnt))
        for n, bx in wb:
            lw = self.last_w.setdefault(n, [])
            lw[:] = [e for e in lw if not self._cover(bx, e[0])]
            lw.append((bx, semname, newcnt))
            rd = self.readers.get(n)
            if rd:
                rd[:] = [e for e in rd if not self._cover(bx, e[0])]

    def dma(self, eng_name, stream, out, in_, r=(), w=(), **kw):
        sns, idx = self.streams[stream]
        qsem = sns[idx % len(sns)]
        self.streams[stream][1] = idx + 1
        rn, wn, waits = self._collect(list(r) + [in_], list(w) + [out])
        waits.append((qsem, self.cnt[qsem]))
        fns = self._wait_fns(waits, eng_name)
        sem = self.sem[qsem]
        newcnt = self.cnt[qsem] + 16
        self.cnt[qsem] = newcnt
        self.nwaits += len(fns)
        self.nops += 1

        def body(eng):
            for s, c in fns:
                eng.wait_ge(s, c)
            eng.dma_start(out=out, in_=in_, **kw).then_inc(sem, 16)

        self.pending[eng_name].append(body)
        self.snap[qsem][newcnt] = dict(self.vc[eng_name], **{qsem: newcnt})
        self._record(rn, wn, qsem, newcnt)

    def barrier(self):
        self.flush()
        targets = [(s, self.cnt[s]) for s in self.cnt if self.cnt[s] > 0]
        for e in self.ENG:
            fns = self._wait_fns(targets, e)
            if not fns:
                continue

            def body(eng, fns=fns):
                for s, c in fns:
                    eng.wait_ge(s, c)
            self.pending[e].append(body)

    def final_wait(self, eng_name, stream):
        sns, _ = self.streams[stream]
        pairs = [(self.sem[sn], self.cnt[sn]) for sn in sns if self.cnt[sn] > 0]

        def body(eng):
            for s, c in pairs:
                eng.wait_ge(s, c)
        self.pending[eng_name].append(body)

    def flush(self):
        for e in self.ENG:
            lst = self.pending[e]
            if not lst:
                continue

            def run(eng, lst=lst):
                for f in lst:
                    f(eng)
            getattr(self.block, self._engmap[e])(run)
            self.pending[e] = []


def build_program(cfg):
    D, S, TOK = cfg["D"], cfg["S"], cfg["TOK"]
    DV, KSEL = cfg["DV"], cfg["KSEL"]
    E, TOPK, I, ISH = cfg["E"], cfg["TOPK"], cfg["I"], cfg["ISH"]
    dbg = set(cfg.get("dbg", ()))
    H = D // DV
    DC = D // 128
    NT = TOK
    NTC = S // 128
    TOKB = min(512, TOK)
    NSB = S // TOKB
    GUF, IC = 2 * I, I // 128
    ISC = ISH // 128
    GUFSC = (2 * ISH) // 128
    NSLOT = 2 * KSEL
    G4 = H // 4
    HPU = 512 // NSLOT
    NU = H // HPU
    UPE = max(1, min(NU, 512 // (HPU * DV)))
    FW = min(512, D)
    FG = _ceil(D, FW)
    NEC = _ceil(D, 512)
    TSUB = _ceil(NT, 128)
    TS128 = min(128, NT)
    HPC = 128 // DV
    G2 = H // 2

    def nspl(n):
        return [(i * 512, min((i + 1) * 512, n)) for i in range(_ceil(n, 512))]

    nc = bass.Bass("TRN2", target_bir_lowering=False, debug=False)

    xT = nc.dram_tensor("xT", [D, S], F32, kind="ExternalInput").ap()
    wqT = nc.dram_tensor("wqT", [128, DC, H], F32, kind="ExternalInput").ap()
    wkT = nc.dram_tensor("wkT", [128, DC, H], F32, kind="ExternalInput").ap()
    wvT = nc.dram_tensor("wvT", [NEC, 128, DC, min(512, D)], F32, kind="ExternalInput").ap()
    woT = nc.dram_tensor("woT", [DC, 128, DC, 128], F32, kind="ExternalInput").ap()
    gwT = nc.dram_tensor("gwT", [128, DC, E], F32, kind="ExternalInput").ap()
    w13T = nc.dram_tensor("w13T", [E, GUF // 128, 128, DC, 128], F32, kind="ExternalInput").ap()
    w2T = nc.dram_tensor("w2T", [E, FG, 128, IC, FW], F32, kind="ExternalInput").ap()
    sw13T = nc.dram_tensor("sw13T", [GUFSC, 128, DC, 128], F32, kind="ExternalInput").ap()
    sw2T = nc.dram_tensor("sw2T", [FG, 128, ISC, FW], F32, kind="ExternalInput").ap()
    out_hbm = nc.dram_tensor("out", [NT, D], F32, kind="ExternalOutput").ap()

    idxrows_dram = nc.dram_tensor("idxrows", [H * NSLOT], F32, kind="Internal").ap()
    q_dram = nc.dram_tensor("q_dram", [H, NT], F32, kind="Internal").ap()
    vgt_dram = nc.dram_tensor("vgt_dram", [128, G2 * 128], mybir.dt.bfloat16, kind="Internal").ap()
    resid_dram = nc.dram_tensor("resid_dram", [DC, 128, NT], F32, kind="Internal").ap()

    DBG = {}
    def dbg_out(name, shape):
        if name in dbg:
            DBG[name] = nc.dram_tensor(name, shape, F32, kind="ExternalOutput").ap()
            return DBG[name]
        return None

    es = contextlib.ExitStack()
    with es:
        b = Builder(nc, es)

        def sb(stack, name, shape, dtype):
            return stack.enter_context(nc.sbuf_tensor(name, shape, dtype))

        def psum(stack, name, shape, dtype=F32):
            return stack.enter_context(nc.psum_tensor(name, shape, dtype))

        q_main = b.new_dma_sem("main", 8)
        q_aux = b.new_dma_sem("aux", 8)
        q_act = b.new_dma_sem("actq", 8)
        q_pool = b.new_dma_sem("poolq", 2)
        q_out = b.new_dma_sem("outq", 4)

        es0 = es
        iota_col = sb(es0, "iota_col", [128, 1], F32)
        iota_i32 = sb(es0, "iota_i32", [128, 1], I32)
        ones_bf = sb(es0, "ones_bf", [128, 128], BF16)
        ident_f32 = sb(es0, "ident_f32", [128, 128], F32)
        ident_bf = sb(es0, "ident_bf", [128, 128], BF16)
        iof = sb(es0, "iof", [128, 128], F32)
        io32 = sb(es0, "io32", [128, 128], I32)
        bdones = sb(es0, "bdones", [128, 4], BF16)
        repmat = sb(es0, "repmat", [128, D], BF16)
        eps_col = sb(es0, "eps_col", [128, 1], F32)
        kc = {"top": sb(es0, "kc_top", [H, KSEL], F32),
              "bot": sb(es0, "kc_bot", [H, KSEL], F32)}
        idxf = sb(es0, "idxf", [H, NSLOT], F32)
        idxu = sb(es0, "idxu", [H, NSLOT], U32)

        b.op("pool", lambda g: g.iota(iota_i32[:], pattern=[[0, 1]], base=0,
                                      channel_multiplier=1), w=[iota_i32])
        b.op("pool", lambda g: g.tensor_copy(iota_col[:], iota_i32[:]),
             r=[iota_i32], w=[iota_col])
        b.op("pool", lambda g: g.iota(io32[:], pattern=[[1, 128]], base=0,
                                      channel_multiplier=0), w=[io32])
        b.op("pool", lambda g: g.tensor_copy(iof[:], io32[:]), r=[io32], w=[iof])
        b.op("pool", lambda g: g.tensor_scalar(out=ident_f32[:], in0=iof[:],
             scalar1=iota_col[:], scalar2=None, op0=ALU.is_equal),
             r=[iof, iota_col], w=[ident_f32])
        b.op("pool", lambda g: g.memset(ones_bf[:], 1.0), w=[ones_bf])
        b.op("pool", lambda g: g.memset(eps_col[:], 1e-5), w=[eps_col])
        b.op("pool", lambda g: g.tensor_scalar(out=ident_bf[:], in0=iof[:],
             scalar1=iota_col[:], scalar2=None, op0=ALU.is_equal),
             r=[iof, iota_col], w=[ident_bf])
        tmp4a = sb(es0, "tmp4a", [128, 4], F32)
        tmp4b = sb(es0, "tmp4b", [128, 4], F32)
        for j in range(4):
            b.op("pool", lambda g, j=j: g.tensor_scalar(
                out=tmp4a[:, j:j + 1], in0=iota_col[:],
                scalar1=float(j * KSEL) - 0.5, scalar2=None, op0=ALU.is_gt),
                r=[iota_col], w=[tmp4a])
            b.op("pool", lambda g, j=j: g.tensor_scalar(
                out=tmp4b[:, j:j + 1], in0=iota_col[:],
                scalar1=float((j + 1) * KSEL) - 0.5, scalar2=None, op0=ALU.is_lt),
                r=[iota_col], w=[tmp4b])
        b.op("pool", lambda g: g.tensor_tensor(out=bdones[:], in0=tmp4a[:],
             in1=tmp4b[:], op=ALU.mult), r=[tmp4a, tmp4b], w=[bdones])
        for ck in range(DC):
            for hl in range(HPC):
                hglob = HPC * ck + hl
                c0 = ck * 128 + hl * DV
                b.op("pool", lambda g, c0=c0, hglob=hglob: g.tensor_scalar(
                    out=repmat[:, c0:c0 + DV],
                    in0=iota_col[:].to_broadcast([128, 1, DV]).squeeze(1),
                    scalar1=float(hglob), scalar2=None, op0=ALU.is_equal),
                    r=[iota_col], w=[repmat])

        esLN = contextlib.ExitStack()
        es.enter_context(esLN)
        x_ln = sb(esLN, "x_ln", [128, DC * S], BF16)

        NSLOTA = 8
        QROT = (("sp", q_main), ("pool", q_aux), ("act", q_act))
        esA = contextlib.ExitStack()
        with esA:
            xf = sb(esA, "xf", [128, NSLOTA * 2 * TOKB], F32)
            sq_bf = sb(esA, "sq_bf", [128, 4 * 2 * TOKB], BF16)
            mu_rep2 = [sb(esA, "mu_rep0", [128, TOKB], F32),
                       sb(esA, "mu_rep1", [128, TOKB], F32)]
            rstd_rep2 = [sb(esA, "rstd_rep0", [128, TOKB], F32),
                         sb(esA, "rstd_rep1", [128, TOKB], F32)]
            var_t = sb(esA, "var_t0", [128, TOKB], F32)
            mu2_t = sb(esA, "mu2_t", [128, TOKB], F32)
            sum_ps2 = [psum(esA, "sum_ps0", [128, TOKB]),
                       psum(esA, "sum_ps1", [128, TOKB])]
            sq_ps2 = [psum(esA, "sq_ps0", [128, TOKB]),
                      psum(esA, "sq_ps1", [128, TOKB])]

            xT3 = xT.rearrange("(a p) s -> p a s", p=128)
            for tb in range(NSB):
                t0 = tb * TOKB
                mu_rep, rstd_rep = mu_rep2[tb % 2], rstd_rep2[tb % 2]
                sum_ps, sq_ps = sum_ps2[tb % 2], sq_ps2[tb % 2]
                for dcp in range(DC // 2):
                    gi = tb * (DC // 2) + dcp
                    slot = gi % NSLOTA
                    eng, q = QROT[gi % 3]
                    sl2 = xf[:, slot * 2 * TOKB:(slot + 1) * 2 * TOKB]
                    b.dma(eng, q,
                          out=sl2.rearrange("p (a s) -> p a s", a=2),
                          in_=xT3[:, 2 * dcp:2 * dcp + 2, t0:t0 + TOKB],
                          w=[sl2])
                    for j in range(2):
                        dc = 2 * dcp + j
                        src = xf[:, (slot * 2 + j) * TOKB:(slot * 2 + j + 1) * TOKB]
                        sq = sq_bf[:, ((gi % 4) * 2 + j) * TOKB:((gi % 4) * 2 + j + 1) * TOKB]
                        dst = x_ln[:, dc * S + t0: dc * S + t0 + TOKB]
                        b.op("dve", lambda v, src=src, dst=dst: v.tensor_copy(dst, src),
                             r=[src], w=[dst])
                        if j == 0:
                            b.op("pool", lambda g, dst=dst, sq=sq: g.tensor_tensor(
                                out=sq, in0=dst, in1=dst, op=ALU.mult), r=[dst], w=[sq])
                        else:
                            b.op("act", lambda a, dst=dst, sq=sq: a.activation(
                                out=sq, in_=dst, func=AF.Square), r=[dst], w=[sq])
                        for n0, n1 in nspl(TOKB):
                            b.op("pe", lambda t, dst=dst, n0=n0, n1=n1, dc=dc, sum_ps=sum_ps: t.matmul(
                                sum_ps[:, n0:n1], ones_bf[:], dst[:, n0:n1],
                                start=(dc == 0), stop=(dc == DC - 1)),
                                r=[ones_bf, dst], w=[sum_ps])
                            b.op("pe", lambda t, sq=sq, n0=n0, n1=n1, dc=dc, sq_ps=sq_ps: t.matmul(
                                sq_ps[:, n0:n1], ones_bf[:], sq[:, n0:n1],
                                start=(dc == 0), stop=(dc == DC - 1)),
                                r=[ones_bf, sq], w=[sq_ps])
                b.op("dve", lambda v, sum_ps=sum_ps, mu_rep=mu_rep: v.tensor_scalar(
                     out=mu_rep[:], in0=sum_ps[:],
                     scalar1=1.0 / D, scalar2=None, op0=ALU.mult),
                     r=[sum_ps], w=[mu_rep])
                b.op("pool", lambda g, mu_rep=mu_rep: g.tensor_tensor(
                     out=mu2_t[:], in0=mu_rep[:],
                     in1=mu_rep[:], op=ALU.mult), r=[mu_rep], w=[mu2_t])
                b.op("dve", lambda v, sq_ps=sq_ps, var_t=var_t: v.tensor_scalar(
                     out=var_t[:], in0=sq_ps[:],
                     scalar1=1.0 / D, scalar2=None, op0=ALU.mult),
                     r=[sq_ps], w=[var_t])
                b.op("dve", lambda v, var_t=var_t: v.tensor_tensor(
                     out=var_t[:], in0=var_t[:],
                     in1=mu2_t[:], op=ALU.subtract), r=[var_t, mu2_t], w=[var_t])
                b.op("act", lambda a, var_t=var_t, rstd_rep=rstd_rep: a.activation(
                     out=rstd_rep[:], in_=var_t[:],
                     func=AF.Ln, bias=eps_col[:], scale=1.0), r=[var_t, eps_col], w=[rstd_rep])
                b.op("act", lambda a, rstd_rep=rstd_rep: a.activation(
                     out=rstd_rep[:], in_=rstd_rep[:],
                     func=AF.Exp, scale=-0.5), r=[rstd_rep], w=[rstd_rep])
                for dc in range(DC):
                    dst = x_ln[:, dc * S + t0: dc * S + t0 + TOKB]
                    b.op("dve" if dc % 2 else "pool",
                         lambda g, dst=dst, mu_rep=mu_rep: g.tensor_tensor(
                         out=dst, in0=dst, in1=mu_rep[:], op=ALU.subtract),
                         r=[dst, mu_rep], w=[dst])
                    b.op("pool" if dc % 2 else "dve",
                         lambda g, dst=dst, rstd_rep=rstd_rep: g.tensor_tensor(
                        out=dst, in0=dst, in1=rstd_rep[:], op=ALU.mult),
                        r=[dst, rstd_rep], w=[dst])
            b.barrier()
        if "dbg_xln" in dbg:
            t = dbg_out("dbg_xln", [128, DC * S])
            with nc.sbuf_tensor("dxl", [128, DC * S], F32) as dxl:
                b.op("dve", lambda v: v.tensor_copy(dxl[:], x_ln[:]), r=[x_ln], w=["dxl"])
                b.dma("sp", q_out, out=t, in_=dxl[:], r=["dxl"])
                b.barrier()

        esBC = contextlib.ExitStack()
        with esBC:
            q_sb = sb(esBC, "q_sb", [H, NT], F32)
            wk_bf = sb(esBC, "wk_bf", [128, DC * H], BF16)
            wq_bf = sb(esBC, "wq_bf", [128, DC * H], BF16)
            keys_sb = sb(esBC, "keys_sb", [H, S], F32)
            work_a = sb(esBC, "work_a", [H, S], F32)
            work_b = sb(esBC, "work_b", [H, S], F32)
            k_ps = psum(esBC, "k_ps", [128 if H > 128 else H, 512])

            for wdst, wsrc, stg in ((wk_bf, wkT, work_a), (wq_bf, wqT, work_b)):
                fsl = stg[:, 0:DC * H]
                b.dma("sp", q_main, out=fsl.rearrange("p (a h) -> p a h", a=DC),
                      in_=wsrc, w=[fsl])
                b.op("dve", lambda v, fsl=fsl, wdst=wdst: v.tensor_copy(
                    wdst[:], fsl), r=[fsl], w=[wdst])

            for s0 in range(0, S, 512):
                s1 = min(s0 + 512, S)
                for dc in range(DC):
                    b.op("pe", lambda t, dc=dc, s0=s0, s1=s1: t.matmul(
                        k_ps[:, 0:s1 - s0],
                        wk_bf[:, dc * H:(dc + 1) * H],
                        x_ln[:, dc * S + s0: dc * S + s1],
                        start=(dc == 0), stop=(dc == DC - 1)),
                        r=[wk_bf, x_ln], w=[k_ps])
                b.op("act", lambda a, s0=s0, s1=s1: a.copy(
                    keys_sb[:, s0:s1], k_ps[:, 0:s1 - s0]), r=[k_ps], w=[keys_sb])
            for s0, s1 in nspl(NT):
                for dc in range(DC):
                    b.op("pe", lambda t, dc=dc, s0=s0, s1=s1: t.matmul(
                        k_ps[:, 0:s1 - s0],
                        wq_bf[:, dc * H:(dc + 1) * H],
                        x_ln[:, dc * S + s0: dc * S + s1],
                        start=(dc == 0), stop=(dc == DC - 1)),
                        r=[wq_bf, x_ln], w=[k_ps])
                b.op("act", lambda a, s0=s0, s1=s1: a.copy(
                    q_sb[:, s0:s1], k_ps[:, 0:s1 - s0]), r=[k_ps], w=[q_sb])
            b.dma("sp", q_main, out=q_dram, in_=q_sb[:], r=[q_sb])

            for si, (setn, src_sign) in enumerate((("top", 1.0), ("bot", -1.0))):
                if src_sign > 0:
                    b.op("dve", lambda v: v.tensor_copy(work_a[:], keys_sb[:]),
                         r=[keys_sb], w=[work_a])
                else:
                    b.op("dve", lambda v: v.tensor_scalar(out=work_a[:],
                         in0=keys_sb[:], scalar1=-1.0, scalar2=None, op0=ALU.mult),
                         r=[keys_sb], w=[work_a])
                cur, nxt = work_a, work_b
                for r8 in range(KSEL // 8):
                    mslice = kc[setn][:, 8 * r8:8 * r8 + 8]
                    islice = idxu[:, si * KSEL + 8 * r8: si * KSEL + 8 * r8 + 8]
                    b.op("dve", lambda v, m=mslice, c=cur: v.max(out=m, in_=c[:]),
                         r=[cur], w=[kc[setn]])
                    b.op("dve", lambda v, m=mslice, i=islice, c=cur: v.max_index(
                        out=i, in_max=m, in_values=c[:]), r=[kc[setn], cur], w=[idxu])
                    if r8 < KSEL // 8 - 1:
                        b.op("dve", lambda v, m=mslice, c=cur, n=nxt: v.match_replace(
                            out=n[:], in_to_replace=m, in_values=c[:], imm_value=-1e30),
                            r=[kc[setn], cur], w=[nxt])
                        cur, nxt = nxt, cur
                if src_sign < 0:
                    b.op("dve", lambda v: v.tensor_scalar(out=kc["bot"][:],
                         in0=kc["bot"][:], scalar1=-1.0, scalar2=None, op0=ALU.mult),
                         r=[kc["bot"]], w=[kc["bot"]])
            b.op("dve", lambda v: v.tensor_copy(idxf[:], idxu[:]), r=[idxu], w=[idxf])
            b.dma("sp", q_main, out=idxrows_dram.rearrange("(h s) -> h s", h=H),
                  in_=idxf[:], r=[idxf])
            if "dbg_keys" in dbg:
                t = dbg_out("dbg_keys", [H, S])
                b.dma("sp", q_out, out=t, in_=keys_sb[:], r=[keys_sb])
            if "dbg_q" in dbg:
                t = dbg_out("dbg_q", [H, NT])
                b.dma("sp", q_out, out=t, in_=q_sb[:], r=[q_sb])
            if "dbg_kc" in dbg:
                t = dbg_out("dbg_kc", [H, NSLOT])
                with nc.sbuf_tensor("dkc", [H, NSLOT], F32) as dkc:
                    b.op("dve", lambda v: v.tensor_copy(dkc[:, 0:KSEL], kc["top"][:]), r=[kc["top"]], w=["dkc"])
                    b.op("dve", lambda v: v.tensor_copy(dkc[:, KSEL:], kc["bot"][:]), r=[kc["bot"]], w=["dkc"])
                    b.dma("sp", q_out, out=t, in_=dkc[:], r=["dkc"])
                    b.barrier()
            if "dbg_idx" in dbg:
                t = dbg_out("dbg_idx", [H, NSLOT])
                b.dma("sp", q_out, out=t, in_=idxf[:], r=[idxf])
            b.barrier()

        esD = contextlib.ExitStack()
        with esD:
            wv_f = sb(esD, "wv_f", [128, 2 * 4 * FW], F32)
            wv_bf = sb(esD, "wv_bf", [128, DC * FW], BF16)
            v_sb = sb(esD, "v_sb", [128, 2 * FW], BF16)
            p_sb = sb(esD, "p_sb", [128, 2 * UPE * 512], BF16)
            idx_bc = sb(esD, "idx_bc", [128, UPE * 512], F32)
            gev = sb(esD, "gev", [128, 512], BF16)
            vgT = sb(esD, "vgT", [128, G2 * 128], BF16)
            v_ps2 = [psum(esD, "v_ps0", [128, FW]), psum(esD, "v_ps1", [128, FW])]
            g_ps = [psum(esD, f"g_ps{i}", [128, HPU * NSLOT]) for i in range(UPE)]
            t_ps = psum(esD, "t_ps", [128, 512], BF16)

            for ec in range(NEC):
                for dg in range(DC // 4):
                    gi = ec * (DC // 4) + dg
                    eng, q = QROT[gi % 3]
                    fsl = wv_f[:, (gi % 2) * 4 * FW:(gi % 2) * 4 * FW + 4 * FW]
                    b.dma(eng, q, out=fsl.rearrange("p (a f) -> p a f", a=4),
                          in_=wvT[ec, :, 4 * dg:4 * dg + 4, :], w=[fsl])
                    for j in range(4):
                        dc = 4 * dg + j
                        b.op("dve", lambda v, fsl=fsl, j=j, dc=dc: v.tensor_copy(
                            wv_bf[:, dc * FW:(dc + 1) * FW],
                            fsl[:, j * FW:(j + 1) * FW]), r=[fsl],
                            w=[wv_bf[:, dc * FW:(dc + 1) * FW]])
                for ul in range(UPE):
                    u = ec * UPE + ul
                    srcap = idxrows_dram[u * HPU * NSLOT:(u + 1) * HPU * NSLOT]
                    b.dma("pool", q_aux,
                          out=idx_bc[:, ul * 512: ul * 512 + HPU * NSLOT],
                          in_=srcap.unsqueeze(0).to_broadcast([128, HPU * NSLOT]),
                          w=[idx_bc])
                for tc in range(NTC):
                    vsl = v_sb[:, (tc % 2) * FW:(tc % 2) * FW + FW]
                    v_ps = v_ps2[tc % 2]
                    for ul in range(UPE):
                        ibc = idx_bc[:, ul * 512: ul * 512 + HPU * NSLOT]
                        psl = p_sb[:, ((tc % 2) * UPE + ul) * 512:
                                   ((tc % 2) * UPE + ul) * 512 + HPU * NSLOT]
                        b.op("dve" if ul % 2 == 0 else "pool",
                             lambda v, ibc=ibc, psl=psl, tc=tc: v.tensor_scalar(
                            out=psl, in0=ibc, scalar1=float(tc * 128),
                            scalar2=iota_col[:], op0=ALU.subtract, op1=ALU.is_equal),
                            r=[ibc, iota_col], w=[psl])
                    for dc in range(DC):
                        b.op("pe", lambda t, dc=dc, tc=tc, v_ps=v_ps: t.matmul(
                            v_ps[:],
                            x_ln[:, dc * S + tc * 128: dc * S + (tc + 1) * 128],
                            wv_bf[:, dc * FW:(dc + 1) * FW],
                            start=(dc == 0), stop=(dc == DC - 1)),
                            r=[x_ln[:, dc * S + tc * 128: dc * S + (tc + 1) * 128],
                               wv_bf[:, dc * FW:(dc + 1) * FW]], w=[v_ps])
                    b.op("dve", lambda v, vsl=vsl, v_ps=v_ps: v.tensor_copy(vsl, v_ps[:]),
                         r=[v_ps], w=[vsl])
                    for ul in range(UPE):
                        psl = p_sb[:, ((tc % 2) * UPE + ul) * 512:
                                   ((tc % 2) * UPE + ul) * 512 + HPU * NSLOT]
                        b.op("pe", lambda t, ul=ul, vsl=vsl, psl=psl, tc=tc: t.matmul(
                            g_ps[ul][:], vsl[:, ul * HPU * DV:(ul + 1) * HPU * DV],
                            psl, start=(tc == 0), stop=(tc == NTC - 1)),
                            r=[vsl, psl], w=[g_ps[ul]])
                for ul in range(UPE):
                    u = ec * UPE + ul
                    b.op("dve", lambda v, ul=ul: v.tensor_copy(gev[:, 0:HPU * NSLOT], g_ps[ul][:]),
                         r=[g_ps[ul]], w=[gev])
                    nblk = (HPU * NSLOT) // 128
                    for j in range(nblk):
                        b.op("pe", lambda t, j=j: t.transpose(
                            t_ps[:, j * 128:(j + 1) * 128],
                            gev[0:HPU * DV, j * 128:(j + 1) * 128], ident_bf[:]),
                            r=[gev, ident_bf], w=[t_ps])
                    for j in range(nblk):
                        g2i = u * nblk + j
                        b.op("dve", lambda v, j=j, g2i=g2i: v.tensor_copy(
                            vgT[:, g2i * 128: g2i * 128 + HPU * DV],
                            t_ps[:, j * 128: j * 128 + HPU * DV]),
                            r=[t_ps], w=[vgT])
                ecw = UPE * ((HPU * NSLOT) // 128) * 128
                b.dma("sp", q_main,
                      out=vgt_dram[:, ec * ecw:(ec + 1) * ecw],
                      in_=vgT[:, ec * ecw:(ec + 1) * ecw],
                      r=[vgT])
            if "dbg_vg" in dbg:
                t = dbg_out("dbg_vg", [128, G2 * 128])
                with nc.sbuf_tensor("dvg", [128, G2 * 128], F32) as dvg:
                    b.op("dve", lambda v: v.tensor_copy(dvg[:], vgT[:]), r=[vgT], w=["dvg"])
                    b.dma("sp", q_out, out=t, in_=dvg[:], r=["dvg"])
                    b.barrier()

            b.barrier()
        b.barrier()
        esLN.close()

        esX = contextlib.ExitStack()
        es.enter_context(esX)
        x_ln2 = sb(esX, "x_ln2", [128, DC * NT], BF16)
        c_dense = sb(esX, "c_dense", [128, TSUB * E], F32)

        esEF = contextlib.ExitStack()
        with esEF:
            xg = sb(esEF, "xg", [128, 2 * 128], BF16)
            q_sb = sb(esEF, "q_sb2", [H, NT], F32)
            q_bf = sb(esEF, "q_bf", [H, NT], BF16)
            G_sb = sb(esEF, "G_sb", [H, NT], F32)
            Gc_sb = sb(esEF, "Gc_sb", [H, NT], F32)
            pvl = {"top": sb(esEF, "pvl_top", [128, G4 * 68], BF16),
                   "bot": sb(esEF, "pvl_bot", [128, G4 * 68], BF16)}
            d_all = {"top": sb(esEF, "d_top", [H, NT], F32),
                     "bot": sb(esEF, "d_bot", [H, NT], F32)}
            stage4 = sb(esEF, "stage4", [4, NT], F32)
            e_sb = sb(esEF, "e_sb", [128, 2 * NT], BF16)
            n_one = sb(esEF, "n_one", [128, DC * NT], BF16)
            n_sb = {"top": n_one, "bot": n_one}
            r_bf = {"top": sb(esEF, "r_top", [H, NT], BF16),
                    "bot": sb(esEF, "r_bot", [H, NT], BF16)}
            aout = sb(esEF, "aout", [128, DC * NT], BF16)
            t1f = sb(esEF, "t1f", [128, NT], F32)
            t2f = t1f
            wo_f = sb(esEF, "wo_f", [128, 2 * DC * 128], F32)
            wo_bf = sb(esEF, "wo_bf", [128, 2 * DC * 128], BF16)
            xo_f = sb(esEF, "xo_f", [128, 2 * NT], F32)
            rst = sb(esEF, "rst", [128, NT], F32)
            rc_t = t1f
            esE1 = contextlib.ExitStack()
            l_ps2 = [psum(esE1, "l_ps0", [128, NT]), psum(esE1, "l_ps1", [128, NT])]
            pv_ps = psum(esE1, "pv_ps", [68, NT])
            rp1 = psum(esE1, "rp1", [128, NT])

            b.dma("sp", q_main, out=q_sb[:], in_=q_dram, w=[q_sb])
            b.op("dve", lambda v: v.tensor_copy(q_bf[:], q_sb[:]), r=[q_sb], w=[q_bf])
            b.op("dve", lambda v: v.tensor_scalar(out=G_sb[:], in0=q_sb[:],
                 scalar1=0.0, scalar2=None, op0=ALU.is_gt), r=[q_sb], w=[G_sb])
            b.op("pool", lambda g: g.tensor_scalar(out=Gc_sb[:], in0=G_sb[:],
                 scalar1=1.0, scalar2=-1.0, op0=ALU.subtract, op1=ALU.mult),
                 r=[G_sb], w=[Gc_sb])
            esE0 = contextlib.ExitStack()
            vgT2 = sb(esE0, "vgT2", [128, G2 * 128], BF16)
            b.dma("sp", q_main, out=vgT2[:], in_=vgt_dram, w=[vgT2])
            b.op("pool", lambda g: g.memset(pvl["top"][:], 0.0), w=[pvl["top"]])
            b.op("pool", lambda g: g.memset(pvl["bot"][:], 0.0), w=[pvl["bot"]])
            for gi in range(G4):
                u = (4 * gi) // HPU
                for si, setn in enumerate(("top", "bot")):
                    for j in range(4):
                        h = 4 * gi + j
                        g2i = h // 2
                        par = h % 2
                        kloc = h - u * HPU
                        src_p0 = 64 * par + 32 * si
                        srcp = vgT2[src_p0:src_p0 + KSEL,
                                    g2i * 128 + kloc * DV: g2i * 128 + (kloc + 1) * DV]
                        dstp = pvl[setn][j * KSEL:(j + 1) * KSEL,
                                         gi * 68 + j * DV: gi * 68 + (j + 1) * DV]
                        if (gi + j) % 2 == 0:
                            b.op("act", lambda a, srcp=srcp, dstp=dstp: a.copy(dstp, srcp),
                                 r=[vgT2], w=[pvl[setn]])
                        else:
                            b.op("pool", lambda g, srcp=srcp, dstp=dstp: g.tensor_copy(dstp, srcp),
                                 r=[vgT2], w=[pvl[setn]])
            for gi in range(G4):
                for setn in ("top", "bot"):
                    b.op("pool", lambda g, gi=gi, setn=setn: g.tensor_copy(
                        pvl[setn][:, gi * 68 + 64: gi * 68 + 68], bdones[:]),
                        r=[bdones], w=[pvl[setn][:, gi * 68 + 64: gi * 68 + 68]])
            b.barrier()
            esE0.close()

            def build_x(si, setn, gi):
                xgs = xg[:, (gi % 2) * 128:(gi % 2) * 128 + 128]
                for j in range(4):
                    h = 4 * gi + j
                    b.op("dve", lambda v, xgs=xgs, j=j, h=h, setn=setn: v.tensor_scalar(
                        out=xgs[0:H, j * KSEL:(j + 1) * KSEL], in0=kc[setn][:],
                        scalar1=ident_f32[0:H, h:h + 1], scalar2=None, op0=ALU.mult),
                        r=[kc[setn], ident_f32], w=[xgs])
                return xgs

            xgs_next = build_x(0, "top", 0)
            for si, setn in enumerate(("top", "bot")):
                for gi in range(G4):
                    xgs = xgs_next
                    if gi + 1 < G4:
                        xgs_next = build_x(si, setn, gi + 1)
                    elif si == 0:
                        xgs_next = build_x(1, "bot", 0)
                    esl = e_sb[:, (gi % 2) * NT:(gi % 2) * NT + NT]
                    l_ps = l_ps2[gi % 2]
                    for n0, n1 in nspl(NT):
                        b.op("pe", lambda t, xgs=xgs, n0=n0, n1=n1: t.matmul(
                            l_ps[:, n0:n1], xgs[0:H, :], q_bf[:, n0:n1],
                            start=True, stop=True), r=[xgs, q_bf], w=[l_ps])
                    b.op("act", lambda a, esl=esl: a.activation(
                        out=esl, in_=l_ps[:], func=AF.Exp), r=[l_ps], w=[esl])
                    for n0, n1 in nspl(NT):
                        b.op("pe", lambda t, esl=esl, n0=n0, n1=n1, setn=setn, gi=gi: t.matmul(
                            pv_ps[:, n0:n1], pvl[setn][:, gi * 68:(gi + 1) * 68],
                            esl[:, n0:n1], start=True, stop=True),
                            r=[pvl[setn], esl], w=[pv_ps])
                    b.op("dve", lambda v: v.tensor_copy(stage4[:], pv_ps[64:68, :]),
                         r=[pv_ps], w=[stage4])
                    b.dma("pool", q_aux, out=d_all[setn][4 * gi:4 * gi + 4, :],
                          in_=stage4[:], w=[d_all[setn]])
                    ck, po = gi // 2, 64 * (gi % 2)
                    b.op("dve", lambda v, ck=ck, po=po, setn=setn: v.tensor_copy(
                        n_sb[setn][po:po + 64, ck * NT:(ck + 1) * NT], pv_ps[0:64, :]),
                        r=[pv_ps], w=[n_sb[setn][po:po + 64, ck * NT:(ck + 1) * NT]])
                gsrc = G_sb if setn == "top" else Gc_sb
                b.op("dve", lambda v, setn=setn: v.reciprocal(out=rc_t[0:H, :],
                     in_=d_all[setn][:]), r=[d_all[setn]], w=[rc_t])
                b.op("dve", lambda v, gsrc=gsrc, setn=setn: v.tensor_tensor(
                    out=r_bf[setn][:], in0=rc_t[0:H, :], in1=gsrc[:], op=ALU.mult),
                    r=[rc_t, gsrc], w=[r_bf[setn]])
                for ck in range(DC):
                    for n0, n1 in nspl(NT):
                        b.op("pe", lambda t, ck=ck, setn=setn, n0=n0, n1=n1: t.matmul(
                            rp1[:, n0:n1], repmat[:, ck * 128:(ck + 1) * 128][0:H, :],
                            r_bf[setn][:, n0:n1], start=True, stop=True),
                            r=[repmat, r_bf[setn]], w=[rp1])
                    if setn == "top":
                        b.op("dve", lambda v, ck=ck: v.tensor_tensor(
                            out=aout[:, ck * NT:(ck + 1) * NT], in0=rp1[:],
                            in1=n_one[:, ck * NT:(ck + 1) * NT], op=ALU.mult),
                            r=[rp1, n_one], w=[aout])
                    else:
                        b.op("dve", lambda v, ck=ck: v.tensor_tensor(
                            out=t1f[:], in0=rp1[:],
                            in1=n_one[:, ck * NT:(ck + 1) * NT], op=ALU.mult),
                            r=[rp1, n_one], w=[t1f])
                        b.op("pool", lambda g, ck=ck: g.tensor_tensor(
                            out=aout[:, ck * NT:(ck + 1) * NT],
                            in0=aout[:, ck * NT:(ck + 1) * NT], in1=t1f[:],
                            op=ALU.add), r=[t1f, aout], w=[aout])

            b.barrier()
            esE1.close()
            esE2 = contextlib.ExitStack()
            wo_ps = psum(esE2, "wo_ps", [128, NT])
            if "dbg_aout" in dbg:
                t = dbg_out("dbg_aout", [128, DC * NT])
                with nc.sbuf_tensor("dao", [128, DC * NT], F32) as dao:
                    b.op("dve", lambda v: v.tensor_copy(dao[:], aout[:]), r=[aout], w=["dao"])
                    b.dma("sp", q_out, out=t, in_=dao[:], r=["dao"])
                    b.barrier()

            for oc in range(DC):
                wsl = wo_bf[:, (oc % 2) * DC * 128:(oc % 2) * DC * 128 + DC * 128]
                fsl = wo_f[:, (oc % 2) * DC * 128:(oc % 2) * DC * 128 + DC * 128]
                b.dma("sp", q_main, out=fsl.rearrange("p (a f) -> p a f", a=DC),
                      in_=woT[oc], w=[fsl])
                b.op("dve", lambda v, fsl=fsl, wsl=wsl: v.tensor_copy(wsl, fsl),
                     r=[fsl], w=[wsl])
                xsl = xo_f[:, (oc % 2) * NT:(oc % 2) * NT + NT]
                b.dma("pool", q_aux, out=xsl, in_=xT[oc * 128:(oc + 1) * 128, 0:NT],
                      w=[xsl])
                for n0, n1 in nspl(NT):
                    for dc in range(DC):
                        b.op("pe", lambda t, wsl=wsl, dc=dc, n0=n0, n1=n1: t.matmul(
                            wo_ps[:, n0:n1], wsl[:, dc * 128:(dc + 1) * 128],
                            aout[:, dc * NT + n0: dc * NT + n1],
                            start=(dc == 0), stop=(dc == DC - 1)),
                            r=[wsl[:, dc * 128:(dc + 1) * 128],
                               aout[:, dc * NT + n0: dc * NT + n1]], w=[wo_ps])
                b.op("dve", lambda v, xsl=xsl: v.tensor_tensor(out=rst[:],
                     in0=wo_ps[:], in1=xsl, op=ALU.add), r=[wo_ps, xsl], w=[rst])
                b.dma("sp", q_main, out=resid_dram[oc], in_=rst[:], r=[rst])
            b.barrier()
            esE2.close()
        b.barrier()

        esGH = contextlib.ExitStack()
        es.enter_context(esGH)
        acc = sb(esGH, "acc", [128, TSUB * D], F32)
        esG = contextlib.ExitStack()
        with esG:
            rs_full = sb(esG, "rs_full", [128, DC * NT], F32)
            sq2_bf = sb(esG, "sq2_bf", [128, 2 * NT], BF16)
            mu_rep = sb(esG, "mu_rep2", [128, NT], F32)
            rstd_rep = sb(esG, "rstd_rep2", [128, NT], F32)
            var_t = sb(esG, "var_t2", [128, NT], F32)
            mu2_t = sb(esG, "mu2_t2", [128, NT], F32)
            gw_f = sb(esG, "gw_f", [128, DC * E], F32)
            gw_bf = sb(esG, "gw_bf", [128, DC * E], BF16)
            sc_sb = sb(esG, "sc_sb", [128, E], F32)
            m8_sb = sb(esG, "m8_sb", [128, 8], F32)
            el_sb = sb(esG, "el_sb", [128, E], F32)
            nlx = sb(esG, "nlx", [128, 1], F32)
            den = sb(esG, "den", [128, 1], F32)
            sum_ps = psum(esG, "sum_ps2", [128, NT])
            sq_ps = psum(esG, "sq_ps2", [128, NT])
            sc_ps = psum(esG, "sc_ps", [128, E])
            tr_ps2 = [psum(esG, "tr_ps0", [128, 128]),
                      psum(esG, "tr_ps1", [128, 128])]

            for dc in range(DC):
                rsl = rs_full[:, dc * NT:(dc + 1) * NT]
                eng, q = QROT[dc % 3]
                b.dma(eng, q, out=rsl, in_=resid_dram[dc], w=[rsl])
                xb = x_ln2[:, dc * NT:(dc + 1) * NT]
                sq = sq2_bf[:, (dc % 2) * NT:(dc % 2) * NT + NT]
                b.op("dve", lambda v, rsl=rsl, xb=xb: v.tensor_copy(xb, rsl),
                     r=[rsl], w=[xb])
                b.op("pool", lambda g, xb=xb, sq=sq: g.tensor_tensor(
                    out=sq, in0=xb, in1=xb, op=ALU.mult), r=[xb], w=[sq])
                for n0, n1 in nspl(NT):
                    b.op("pe", lambda t, xb=xb, n0=n0, n1=n1, dc=dc: t.matmul(
                        sum_ps[:, n0:n1], ones_bf[:], xb[:, n0:n1],
                        start=(dc == 0), stop=(dc == DC - 1)),
                        r=[ones_bf, xb], w=[sum_ps])
                    b.op("pe", lambda t, sq=sq, n0=n0, n1=n1, dc=dc: t.matmul(
                        sq_ps[:, n0:n1], ones_bf[:], sq[:, n0:n1],
                        start=(dc == 0), stop=(dc == DC - 1)),
                        r=[ones_bf, sq], w=[sq_ps])
            b.op("dve", lambda v: v.tensor_scalar(out=mu_rep[:], in0=sum_ps[:],
                 scalar1=1.0 / D, scalar2=None, op0=ALU.mult), r=[sum_ps], w=[mu_rep])
            b.op("pool", lambda g: g.tensor_tensor(out=mu2_t[:], in0=mu_rep[:],
                 in1=mu_rep[:], op=ALU.mult), r=[mu_rep], w=[mu2_t])
            b.op("dve", lambda v: v.tensor_scalar(out=var_t[:], in0=sq_ps[:],
                 scalar1=1.0 / D, scalar2=None, op0=ALU.mult), r=[sq_ps], w=[var_t])
            b.op("dve", lambda v: v.tensor_tensor(out=var_t[:], in0=var_t[:],
                 in1=mu2_t[:], op=ALU.subtract), r=[var_t, mu2_t], w=[var_t])
            b.op("act", lambda a: a.activation(out=rstd_rep[:], in_=var_t[:],
                 func=AF.Ln, bias=eps_col[:], scale=1.0), r=[var_t, eps_col], w=[rstd_rep])
            b.op("act", lambda a: a.activation(out=rstd_rep[:], in_=rstd_rep[:],
                 func=AF.Exp, scale=-0.5), r=[rstd_rep], w=[rstd_rep])
            for dc in range(DC):
                xb = x_ln2[:, dc * NT:(dc + 1) * NT]
                b.op("dve" if dc % 2 else "pool",
                     lambda g, xb=xb: g.tensor_tensor(out=xb, in0=xb,
                     in1=mu_rep[:], op=ALU.subtract), r=[xb, mu_rep], w=[xb])
                b.op("pool" if dc % 2 else "dve",
                     lambda g, xb=xb: g.tensor_tensor(
                    out=xb, in0=xb, in1=rstd_rep[:],
                    op=ALU.mult), r=[xb, rstd_rep], w=[xb])
            for ck in range(DC):
                for ts in range(TSUB):
                    tr_ps = tr_ps2[(ck * TSUB + ts) % 2]
                    b.op("pe", lambda t, ck=ck, ts=ts, tr_ps=tr_ps: t.transpose(
                        tr_ps[0:TS128, :],
                        rs_full[:, ck * NT + ts * TS128: ck * NT + (ts + 1) * TS128],
                        ident_f32[:]), r=[rs_full, ident_f32], w=[tr_ps])
                    if ts % 2:
                        b.op("dve", lambda v, ts=ts, ck=ck, tr_ps=tr_ps: v.tensor_copy(
                            acc[0:TS128, ts * D + ck * 128: ts * D + (ck + 1) * 128],
                            tr_ps[0:TS128, :]), r=[tr_ps], w=[acc])
                    else:
                        b.op("act", lambda a, ts=ts, ck=ck, tr_ps=tr_ps: a.copy(
                            acc[0:TS128, ts * D + ck * 128: ts * D + (ck + 1) * 128],
                            tr_ps[0:TS128, :]), r=[tr_ps], w=[acc])
            if "dbg_xln2" in dbg:
                t = dbg_out("dbg_xln2", [128, DC * NT])
                with nc.sbuf_tensor("dx2", [128, DC * NT], F32) as dx2:
                    b.op("dve", lambda v: v.tensor_copy(dx2[:], x_ln2[:]), r=[x_ln2], w=["dx2"])
                    b.dma("sp", q_out, out=t, in_=dx2[:], r=["dx2"])
                    b.barrier()

            b.dma("sp", q_main, out=gw_f[:].rearrange("p (dc e) -> p dc e", dc=DC),
                  in_=gwT, w=[gw_f])
            b.op("dve", lambda v: v.tensor_copy(gw_bf[:], gw_f[:]), r=[gw_f], w=[gw_bf])
            TP = TS128
            for ts in range(TSUB):
                for dc in range(DC):
                    b.op("pe", lambda t, ts=ts, dc=dc: t.matmul(
                        sc_ps[0:TP, :],
                        x_ln2[:, dc * NT + ts * TP: dc * NT + ts * TP + TP],
                        gw_bf[:, dc * E:(dc + 1) * E],
                        start=(dc == 0), stop=(dc == DC - 1)),
                        r=[x_ln2, gw_bf], w=[sc_ps])
                b.op("dve", lambda v: v.tensor_copy(sc_sb[0:TP, :], sc_ps[0:TP, :]),
                     r=[sc_ps], w=[sc_sb])
                b.op("dve", lambda v: v.max(out=m8_sb[0:TP, :], in_=sc_sb[0:TP, :]),
                     r=[sc_sb], w=[m8_sb])
                b.op("dve", lambda v: v.tensor_scalar(out=nlx[0:TP, :], in0=m8_sb[0:TP, 0:1],
                     scalar1=-1.0, scalar2=None, op0=ALU.mult), r=[m8_sb], w=[nlx])
                b.op("act", lambda a: a.activation(out=el_sb[0:TP, :], in_=sc_sb[0:TP, :],
                     func=AF.Exp, bias=nlx[0:TP, :], scale=1.0), r=[sc_sb, nlx], w=[el_sb])
                b.op("dve", lambda v: v.tensor_scalar(out=sc_sb[0:TP, :], in0=sc_sb[0:TP, :],
                     scalar1=m8_sb[0:TP, TOPK - 1:TOPK], scalar2=None, op0=ALU.is_ge),
                     r=[sc_sb, m8_sb], w=[sc_sb])
                b.op("dve", lambda v: v.tensor_tensor(out=el_sb[0:TP, :], in0=el_sb[0:TP, :],
                     in1=sc_sb[0:TP, :], op=ALU.mult), r=[el_sb, sc_sb], w=[el_sb])
                b.op("dve", lambda v: v.reduce_sum(out=den[0:TP, :], in_=el_sb[0:TP, :],
                     axis=AX.X), r=[el_sb], w=[den])
                b.op("dve", lambda v: v.reciprocal(out=den[0:TP, :], in_=den[0:TP, :]),
                     r=[den], w=[den])
                b.op("dve", lambda v, ts=ts: v.tensor_scalar(
                    out=c_dense[0:TP, ts * E:(ts + 1) * E], in0=el_sb[0:TP, :],
                    scalar1=den[0:TP, :], scalar2=None, op0=ALU.mult),
                    r=[el_sb, den], w=[c_dense])
            if "dbg_c" in dbg:
                t = dbg_out("dbg_c", [128, TSUB * E])
                b.dma("sp", q_out, out=t, in_=c_dense[:], r=[c_dense])
            b.barrier()

        esH = contextlib.ExitStack()
        with esH:
            NTH = NT // 2 if NT >= 256 else NT
            h_sb = sb(esH, "h_sb", [128, max(IC * NT, ISC * NTH)], BF16)
            w13g_f = sb(esH, "w13g_f", [128, 2 * DC * 128], F32)
            w13u_f = sb(esH, "w13u_f", [128, 2 * DC * 128], F32)
            w13g_bf = sb(esH, "w13g_bf", [128, 2 * DC * 128], BF16)
            w13u_bf = sb(esH, "w13u_bf", [128, 2 * DC * 128], BF16)
            W2G = 3
            w2_f = sb(esH, "w2_f", [128, 2 * W2G * FW], F32)
            w2_bf = sb(esH, "w2_bf", [128, 2 * W2G * FW], BF16)
            silu_bf = sb(esH, "silu_bf", [128, NT], BF16)
            ps8 = [psum(esH, f"ps8_{i}", [128, 512]) for i in range(8)]
            gg2 = [ps8[0], ps8[1]]
            uu2 = [ps8[2], ps8[3]]

            def w13_slots(icp):
                par2 = icp % 2
                W13W = DC * 128
                return (w13g_bf[:, par2 * W13W: par2 * W13W + W13W],
                        w13u_bf[:, par2 * W13W: par2 * W13W + W13W],
                        w13g_f[:, par2 * W13W: par2 * W13W + W13W],
                        w13u_f[:, par2 * W13W: par2 * W13W + W13W])

            def w13_load(icp, wsrc, gufc_half):
                gslot, uslot, gfs, ufs = w13_slots(icp)
                b.dma("sp", q_main, out=gfs.rearrange("p (dc f) -> p dc f", dc=DC),
                      in_=wsrc[icp], w=[gfs])
                b.op("dve", lambda v, gslot=gslot, gfs=gfs: v.tensor_copy(gslot, gfs),
                     r=[gfs], w=[gslot])
                b.dma("pool", q_aux, out=ufs.rearrange("p (dc f) -> p dc f", dc=DC),
                      in_=wsrc[icp + gufc_half], w=[ufs])
                b.op("act", lambda a, uslot=uslot, ufs=ufs: a.copy(uslot, ufs),
                     r=[ufs], w=[uslot])

            def gu_phase(nexp, gufc_half, wsrc, ntl, first):
                if first:
                    w13_load(0, wsrc, gufc_half)
                for icp in range(gufc_half):
                    gslot, uslot, gfs, ufs = w13_slots(icp)
                    if icp + 1 < gufc_half:
                        w13_load(icp + 1, wsrc, gufc_half)
                    for hh, (n0, n1) in enumerate(nspl(ntl)):
                        par = (icp * len(nspl(ntl)) + hh) % 2
                        gp, up = gg2[par], uu2[par]
                        nw = n1 - n0
                        for dc in range(DC):
                            b.op("pe", lambda t, gslot=gslot, dc=dc, n0=n0, nw=nw, gp=gp: t.matmul(
                                gp[:, 0:nw], gslot[:, dc * 128:(dc + 1) * 128],
                                x_ln2[:, dc * NT + n0: dc * NT + n0 + nw],
                                start=(dc == 0), stop=(dc == DC - 1)),
                                r=[gslot, x_ln2], w=[gp])
                        for dc in range(DC):
                            b.op("pe", lambda t, uslot=uslot, dc=dc, n0=n0, nw=nw, up=up: t.matmul(
                                up[:, 0:nw], uslot[:, dc * 128:(dc + 1) * 128],
                                x_ln2[:, dc * NT + n0: dc * NT + n0 + nw],
                                start=(dc == 0), stop=(dc == DC - 1)),
                                r=[uslot, x_ln2], w=[up])
                        b.op("act", lambda a, n0=n0, nw=nw, gp=gp: a.activation(
                            out=silu_bf[:, n0:n0 + nw], in_=gp[:, 0:nw], func=AF.Silu),
                            r=[gp], w=[silu_bf[:, n0:n0 + nw]])
                        b.op("dve", lambda v, icp=icp, ntl=ntl, n0=n0, nw=nw, up=up: v.tensor_tensor(
                            out=h_sb[:, icp * ntl + n0: icp * ntl + n0 + nw],
                            in0=silu_bf[:, n0:n0 + nw], in1=up[:, 0:nw], op=ALU.mult),
                            r=[silu_bf[:, n0:n0 + nw], up],
                            w=[h_sb[:, icp * ntl + n0: icp * ntl + n0 + nw]])

            def y_phase(ic_n, w2src, ntl, toff, cslot):
                tsn = _ceil(ntl, 128)
                groups = []
                ic0 = 0
                while ic0 < ic_n:
                    g = min(W2G, ic_n - ic0)
                    groups.append((ic0, g))
                    ic0 += g
                def w2_load(fg, gidx):
                    ic0, g = groups[gidx]
                    par = (fg * len(groups) + gidx) % 2
                    wfs = w2_f[:, par * W2G * FW: par * W2G * FW + g * FW]
                    wslot = w2_bf[:, par * W2G * FW: par * W2G * FW + g * FW]
                    b.dma("sp", q_main, out=wfs.rearrange("p (g f) -> p g f", g=g),
                          in_=w2src[fg, :, ic0:ic0 + g, :], w=[wfs])
                    b.op("act", lambda a, wslot=wslot, wfs=wfs: a.copy(
                        wslot, wfs), r=[wfs], w=[wslot])
                    return wslot

                wslot_next = w2_load(0, 0)
                for fg in range(FG):
                    for gidx, (ic0, g) in enumerate(groups):
                        wslot = wslot_next
                        if gidx + 1 < len(groups):
                            wslot_next = w2_load(fg, gidx + 1)
                        elif fg + 1 < FG:
                            wslot_next = w2_load(fg + 1, 0)
                        for gg in range(g):
                            ic = ic0 + gg
                            for ts in range(tsn):
                                tt0 = toff + ts * 128
                                tw = min(128, toff + ntl - tt0)
                                b.op("pe", lambda t, ic=ic, gg=gg, ts=ts, wslot=wslot,
                                     tt0=tt0, tw=tw, ic_n=ic_n, ntl=ntl: t.matmul(
                                    ps8[ts][0:tw, 0:FW],
                                    h_sb[:, ic * ntl + tt0 - toff: ic * ntl + tt0 - toff + tw],
                                    wslot[:, gg * FW:(gg + 1) * FW],
                                    start=(ic == 0), stop=(ic == ic_n - 1)),
                                    r=[h_sb[:, ic * ntl + tt0 - toff: ic * ntl + tt0 - toff + tw],
                                       wslot[:, gg * FW:(gg + 1) * FW]], w=[ps8[ts]])
                    for ts in range(tsn):
                        tt0 = toff + ts * 128
                        tw = min(128, toff + ntl - tt0)
                        tsg = tt0 // 128
                        asl = acc[0:tw, tsg * D + fg * FW: tsg * D + (fg + 1) * FW]
                        if cslot is None:
                            b.op("dve", lambda v, ts=ts, tw=tw, asl=asl: v.tensor_tensor(
                                out=asl, in0=ps8[ts][0:tw, 0:FW], in1=asl,
                                op=ALU.add), r=[ps8[ts], asl], w=[asl])
                        else:
                            b.op("dve", lambda v, ts=ts, tsg=tsg, tw=tw, cslot=cslot, asl=asl: v.scalar_tensor_tensor(
                                out=asl, in0=ps8[ts][0:tw, 0:FW],
                                scalar=c_dense[0:tw, tsg * E + cslot: tsg * E + cslot + 1],
                                in1=asl, op0=ALU.mult, op1=ALU.add),
                                r=[ps8[ts], asl, c_dense], w=[asl])

            for e in range(E):
                gu_phase(e, IC, w13T[e], NT, first=(e == 0))
                if e + 1 < E:
                    w13_load(0, w13T[e + 1], IC)
                y_phase(IC, w2T[e], NT, 0, e)
            for th in range(NT // NTH):
                gu_sh = lambda icp: None
                if th == 0:
                    w13_load(0, sw13T, ISC)
                for icp in range(ISC):
                    gslot, uslot, gfs, ufs = w13_slots(icp)
                    if icp + 1 < ISC:
                        w13_load(icp + 1, sw13T, ISC)
                    elif th + 1 < NT // NTH:
                        w13_load(0, sw13T, ISC)
                    t0 = th * NTH
                    for hh, (n0, n1) in enumerate(nspl(NTH)):
                        par = (icp * len(nspl(NTH)) + hh) % 2
                        gp, up = gg2[par], uu2[par]
                        nw = n1 - n0
                        for dc in range(DC):
                            b.op("pe", lambda t, gslot=gslot, dc=dc, n0=n0, nw=nw, t0=t0, gp=gp: t.matmul(
                                gp[:, 0:nw], gslot[:, dc * 128:(dc + 1) * 128],
                                x_ln2[:, dc * NT + t0 + n0: dc * NT + t0 + n0 + nw],
                                start=(dc == 0), stop=(dc == DC - 1)),
                                r=[gslot, x_ln2], w=[gp])
                        for dc in range(DC):
                            b.op("pe", lambda t, uslot=uslot, dc=dc, n0=n0, nw=nw, t0=t0, up=up: t.matmul(
                                up[:, 0:nw], uslot[:, dc * 128:(dc + 1) * 128],
                                x_ln2[:, dc * NT + t0 + n0: dc * NT + t0 + n0 + nw],
                                start=(dc == 0), stop=(dc == DC - 1)),
                                r=[uslot, x_ln2], w=[up])
                        b.op("act", lambda a, n0=n0, nw=nw, gp=gp: a.activation(
                            out=silu_bf[:, n0:n0 + nw], in_=gp[:, 0:nw], func=AF.Silu),
                            r=[gp], w=[silu_bf[:, n0:n0 + nw]])
                        b.op("dve", lambda v, icp=icp, n0=n0, nw=nw, up=up: v.tensor_tensor(
                            out=h_sb[:, icp * NTH + n0: icp * NTH + n0 + nw],
                            in0=silu_bf[:, n0:n0 + nw], in1=up[:, 0:nw], op=ALU.mult),
                            r=[silu_bf[:, n0:n0 + nw], up],
                            w=[h_sb[:, icp * NTH + n0: icp * NTH + n0 + nw]])
                y_phase(ISC, sw2T, NTH, th * NTH, None)

            for ts in range(TSUB):
                b.dma("sp", q_out,
                      out=out_hbm[ts * TS128:(ts + 1) * TS128, :],
                      in_=acc[0:TS128, ts * D:(ts + 1) * D], r=[acc])
            b.final_wait("sp", q_out)
            if dbg:
                b.barrier()
        b.flush()
    return nc, DBG



def _prep_weights(cfg, Wq, Wk, Wv, Wo, gw, w13, w2, sw13, sw2):
    D, E, I, ISH = cfg["D"], cfg["E"], cfg["I"], cfg["ISH"]
    DC = D // 128
    FW = min(512, D)
    NEC = _ceil(D, 512)
    FG = _ceil(D, FW)
    IC = I // 128
    ISC = ISH // 128
    GUFC = (2 * I) // 128
    GUFSC = (2 * ISH) // 128
    H = D // cfg["DV"]
    c = np.ascontiguousarray
    out = {}
    out["wqT"] = c(Wq.T.reshape(DC, 128, H).transpose(1, 0, 2))
    out["wkT"] = c(Wk.T.reshape(DC, 128, H).transpose(1, 0, 2))
    out["wvT"] = c(Wv.T.reshape(DC, 128, NEC, FW).transpose(2, 1, 0, 3))
    out["woT"] = c(Wo.T.reshape(DC, 128, DC, 128).transpose(2, 1, 0, 3))
    out["gwT"] = c(gw.T.reshape(DC, 128, E).transpose(1, 0, 2))
    out["w13T"] = c(w13.transpose(0, 2, 1).reshape(E, DC, 128, GUFC, 128)
                    .transpose(0, 3, 2, 1, 4))
    out["w2T"] = c(w2.transpose(0, 2, 1).reshape(E, IC, 128, FG, FW)
                   .transpose(0, 3, 2, 1, 4))
    out["sw13T"] = c(sw13.T.reshape(DC, 128, GUFSC, 128).transpose(2, 1, 0, 3))
    out["sw2T"] = c(sw2.T.reshape(ISC, 128, FG, FW).transpose(2, 1, 0, 3))
    return out


def host_inmaps(cfg, x, Wq, Wk, Wv, Wo, gw, w13, w2, sw13, sw2):
    D, S, TOK = cfg["D"], cfg["S"], cfg["TOK"]
    CPB = S // TOK
    w = _prep_weights(cfg, Wq, Wk, Wv, Wo, gw, w13, w2, sw13, sw2)
    in_maps = []
    for core in range(CORES):
        bb, q = core // CPB, core % CPB
        xb = x[bb].T
        perm = np.concatenate([np.arange(q * TOK, (q + 1) * TOK),
                               np.arange(0, q * TOK),
                               np.arange((q + 1) * TOK, S)])
        in_maps.append(dict(xT=np.ascontiguousarray(xb[:, perm]), **w))
    return in_maps


def assemble_out(cfg, results, B):
    D, S, TOK = cfg["D"], cfg["S"], cfg["TOK"]
    CPB = S // TOK
    out = np.empty((B, S, D), np.float32)
    for core in range(CORES):
        bb, q = core // CPB, core % CPB
        out[bb, q * TOK:(q + 1) * TOK, :] = results[core]["out"]
    return out


def kernel(L0_attn_x, L0_attn_Wq, L0_attn_Wk, L0_attn_Wv, L0_attn_Wo,
           L0_ffn_gate_w, L0_ffn_w13, L0_ffn_w2, L0_ffn_sw13, L0_ffn_sw2):
    cfg = CFG_FULL
    x = np.ascontiguousarray(np.asarray(L0_attn_x, dtype=np.float32))
    args = [np.asarray(a, dtype=np.float32) for a in (
        L0_attn_Wq, L0_attn_Wk, L0_attn_Wv, L0_attn_Wo, L0_ffn_gate_w,
        L0_ffn_w13, L0_ffn_w2, L0_ffn_sw13, L0_ffn_sw2)]
    nc, _ = build_program(cfg)
    in_maps = host_inmaps(cfg, x, *args)
    res = run_bass_kernel_spmd(nc, in_maps, core_ids=list(range(CORES)))
    return assemble_out(cfg, res.results, x.shape[0])


# version 19
# speedup vs baseline: 1.1994x; 1.1994x over previous
import math
import contextlib
import numpy as np

import concourse.bass as bass
import concourse.mybir as mybir
from concourse.bass_utils import run_bass_kernel_spmd

F32 = mybir.dt.float32
BF16 = mybir.dt.bfloat16
U32 = mybir.dt.uint32
I32 = mybir.dt.int32
AF = mybir.ActivationFunctionType
ALU = mybir.AluOpType
AX = mybir.AxisListType

CORES = 8

CFG_FULL = dict(D=2048, S=4096, TOK=1024, DV=16, KSEL=32,
                E=16, TOPK=6, I=1408, ISH=2816, dbg=())
CFG_TINY = dict(D=512, S=256, TOK=64, DV=16, KSEL=32,
                E=8, TOPK=3, I=256, ISH=512, dbg=())


def _ceil(a, b):
    return (a + b - 1) // b


class Builder:

    ENG = ("pe", "dve", "act", "pool", "sp")

    def __init__(self, nc, es):
        self.nc = nc
        self.block = es.enter_context(nc.Block())
        self.es = es
        self.sem = {}
        self.cnt = {}
        self.vc = {e: {} for e in self.ENG}
        self.snap = {}
        for e in self.ENG:
            self.sem[e] = es.enter_context(nc.semaphore(f"sem_{e}"))
            self.cnt[e] = 0
            self.snap[e] = {}
        self.last_w = {}
        self.readers = {}
        self._engmap = dict(pe="tensor", dve="vector", act="scalar",
                            pool="gpsimd", sp="sync")
        self.nwaits = 0
        self.nops = 0
        self.pending = {e: [] for e in self.ENG}

    def new_dma_sem(self, name, nslots=4):
        sns = []
        for i in range(nslots):
            sn = f"dq_{name}{i}"
            self.sem[sn] = self.es.enter_context(self.nc.semaphore(sn))
            self.cnt[sn] = 0
            self.snap[sn] = {}
            sns.append(sn)
        self.streams = getattr(self, "streams", {})
        self.streams[name] = [sns, 0]
        return name

    FULL = (0, 1 << 62, 0, 1 << 62)

    @classmethod
    def _box(cls, a):
        if a is None:
            return None
        if isinstance(a, str):
            return (a, cls.FULL)
        ap = getattr(a, "ap", None)
        t = getattr(a, "tensor", None)
        if t is None:
            return (a.name, cls.FULL)
        name = t.name
        try:
            import concourse.bass_types as _bt
            sbufish = a.space != _bt.MemorySpace.DRAM
        except Exception:
            sbufish = True
        dims = list(ap)
        off = a.offset
        if not sbufish or not dims:
            span = sum(max(0, s) * (c - 1) for s, c in dims) + 1
            return (name, (0, 1, off, off + span))
        row = 1
        for d in t.shape[1:]:
            row *= d
        p0 = off // row if row else 0
        f0 = off % row if row else 0
        pcnt = dims[0][1]
        fspan = sum(max(0, s) * (c - 1) for s, c in dims[1:]) + 1
        return (name, (p0, p0 + pcnt, f0, f0 + fspan))

    @staticmethod
    def _ovl(b1, b2):
        return b1[0] < b2[1] and b2[0] < b1[1] and b1[2] < b2[3] and b2[2] < b1[3]

    def _collect(self, r, w):
        waits = []
        rb = [self._box(a) for a in r if a is not None]
        wb = [self._box(a) for a in w if a is not None]
        for n, bx in rb:
            for obx, sem, cnt in self.last_w.get(n, ()):
                if self._ovl(bx, obx):
                    waits.append((sem, cnt))
        for n, bx in wb:
            for obx, sem, cnt in self.last_w.get(n, ()):
                if self._ovl(bx, obx):
                    waits.append((sem, cnt))
            for obx, sem, cnt in self.readers.get(n, ()):
                if self._ovl(bx, obx):
                    waits.append((sem, cnt))
        return rb, wb, waits

    def _wait_fns(self, waits, eng_name):
        fns = []
        myvc = self.vc[eng_name]
        for semname, count in waits:
            if semname.startswith("dq_"):
                count = self.cnt[semname]
            if count <= 0 or myvc.get(semname, 0) >= count:
                continue
            s = self.sem[semname]
            fns.append((s, count))
            snapd = self.snap[semname]
            best = snapd.get(count)
            if best is None:
                ks = [k for k in snapd if k >= count]
                best = snapd[min(ks)] if ks else (snapd[max(snapd)] if snapd else None)
            if best:
                for k, v in best.items():
                    if k.startswith("dq_"):
                        continue
                    if myvc.get(k, 0) < v:
                        myvc[k] = v
            if myvc.get(semname, 0) < count:
                myvc[semname] = count
        return fns

    def op(self, eng_name, fn, r=(), w=()):
        rn, wn, waits = self._collect(r, w)
        if eng_name == "pe":
            waits = [wt for wt in waits if wt[0] != "pe"]
        fns = self._wait_fns(waits, eng_name)
        sem = self.sem[eng_name]
        newcnt = self.cnt[eng_name] + 1
        self.cnt[eng_name] = newcnt
        self.nwaits += len(fns)
        self.nops += 1

        def body(eng):
            for s, c in fns:
                eng.wait_ge(s, c)
            fn(eng).then_inc(sem, 1)

        self.pending[eng_name].append(body)
        self.snap[eng_name][newcnt] = dict(self.vc[eng_name], **{eng_name: newcnt})
        self._record(rn, wn, eng_name, newcnt)

    @staticmethod
    def _cover(bx, obx):
        return (bx[0] <= obx[0] and obx[1] <= bx[1]
                and bx[2] <= obx[2] and obx[3] <= bx[3])

    def _record(self, rb, wb, semname, newcnt):
        for n, bx in rb:
            self.readers.setdefault(n, []).append((bx, semname, newcnt))
        for n, bx in wb:
            lw = self.last_w.setdefault(n, [])
            lw[:] = [e for e in lw if not self._cover(bx, e[0])]
            lw.append((bx, semname, newcnt))
            rd = self.readers.get(n)
            if rd:
                rd[:] = [e for e in rd if not self._cover(bx, e[0])]

    def dma(self, eng_name, stream, out, in_, r=(), w=(), **kw):
        sns, idx = self.streams[stream]
        qsem = sns[idx % len(sns)]
        self.streams[stream][1] = idx + 1
        rn, wn, waits = self._collect(list(r) + [in_], list(w) + [out])
        waits.append((qsem, self.cnt[qsem]))
        fns = self._wait_fns(waits, eng_name)
        sem = self.sem[qsem]
        newcnt = self.cnt[qsem] + 16
        self.cnt[qsem] = newcnt
        self.nwaits += len(fns)
        self.nops += 1

        def body(eng):
            for s, c in fns:
                eng.wait_ge(s, c)
            eng.dma_start(out=out, in_=in_, **kw).then_inc(sem, 16)

        self.pending[eng_name].append(body)
        self.snap[qsem][newcnt] = dict(self.vc[eng_name], **{qsem: newcnt})
        self._record(rn, wn, qsem, newcnt)

    def barrier(self):
        self.flush()
        targets = [(s, self.cnt[s]) for s in self.cnt if self.cnt[s] > 0]
        for e in self.ENG:
            fns = self._wait_fns(targets, e)
            if not fns:
                continue

            def body(eng, fns=fns):
                for s, c in fns:
                    eng.wait_ge(s, c)
            self.pending[e].append(body)

    def final_wait(self, eng_name, stream):
        sns, _ = self.streams[stream]
        pairs = [(self.sem[sn], self.cnt[sn]) for sn in sns if self.cnt[sn] > 0]

        def body(eng):
            for s, c in pairs:
                eng.wait_ge(s, c)
        self.pending[eng_name].append(body)

    def flush(self):
        for e in self.ENG:
            lst = self.pending[e]
            if not lst:
                continue

            def run(eng, lst=lst):
                for f in lst:
                    f(eng)
            getattr(self.block, self._engmap[e])(run)
            self.pending[e] = []


def build_program(cfg):
    D, S, TOK = cfg["D"], cfg["S"], cfg["TOK"]
    DV, KSEL = cfg["DV"], cfg["KSEL"]
    E, TOPK, I, ISH = cfg["E"], cfg["TOPK"], cfg["I"], cfg["ISH"]
    dbg = set(cfg.get("dbg", ()))
    H = D // DV
    DC = D // 128
    NT = TOK
    NTC = S // 128
    TOKB = min(512, TOK)
    NSB = S // TOKB
    GUF, IC = 2 * I, I // 128
    ISC = ISH // 128
    GUFSC = (2 * ISH) // 128
    NSLOT = 2 * KSEL
    G4 = H // 4
    HPU = 512 // NSLOT
    NU = H // HPU
    UPE = max(1, min(NU, 512 // (HPU * DV)))
    FW = min(512, D)
    FG = _ceil(D, FW)
    NEC = _ceil(D, 512)
    TSUB = _ceil(NT, 128)
    TS128 = min(128, NT)
    HPC = 128 // DV
    G2 = H // 2

    def nspl(n):
        return [(i * 512, min((i + 1) * 512, n)) for i in range(_ceil(n, 512))]

    nc = bass.Bass("TRN2", target_bir_lowering=False, debug=False)

    xT = nc.dram_tensor("xT", [D, S], F32, kind="ExternalInput").ap()
    wqT = nc.dram_tensor("wqT", [128, DC, H], F32, kind="ExternalInput").ap()
    wkT = nc.dram_tensor("wkT", [128, DC, H], F32, kind="ExternalInput").ap()
    wvT = nc.dram_tensor("wvT", [NEC, 128, DC, min(512, D)], F32, kind="ExternalInput").ap()
    woT = nc.dram_tensor("woT", [DC, 128, DC, 128], F32, kind="ExternalInput").ap()
    gwT = nc.dram_tensor("gwT", [128, DC, E], F32, kind="ExternalInput").ap()
    w13T = nc.dram_tensor("w13T", [E, GUF // 128, 128, DC, 128], F32, kind="ExternalInput").ap()
    w2T = nc.dram_tensor("w2T", [E, FG, 128, IC, FW], F32, kind="ExternalInput").ap()
    sw13T = nc.dram_tensor("sw13T", [GUFSC, 128, DC, 128], F32, kind="ExternalInput").ap()
    sw2T = nc.dram_tensor("sw2T", [FG, 128, ISC, FW], F32, kind="ExternalInput").ap()
    out_hbm = nc.dram_tensor("out", [NT, D], F32, kind="ExternalOutput").ap()

    idxrows_dram = nc.dram_tensor("idxrows", [H * NSLOT], F32, kind="Internal").ap()
    q_dram = nc.dram_tensor("q_dram", [H, NT], F32, kind="Internal").ap()
    vgt_dram = nc.dram_tensor("vgt_dram", [128, G2 * 128], mybir.dt.bfloat16, kind="Internal").ap()
    resid_dram = nc.dram_tensor("resid_dram", [DC, 128, NT], F32, kind="Internal").ap()

    DBG = {}
    def dbg_out(name, shape):
        if name in dbg:
            DBG[name] = nc.dram_tensor(name, shape, F32, kind="ExternalOutput").ap()
            return DBG[name]
        return None

    es = contextlib.ExitStack()
    with es:
        b = Builder(nc, es)

        def sb(stack, name, shape, dtype):
            return stack.enter_context(nc.sbuf_tensor(name, shape, dtype))

        def psum(stack, name, shape, dtype=F32):
            return stack.enter_context(nc.psum_tensor(name, shape, dtype))

        q_main = b.new_dma_sem("main", 8)
        q_aux = b.new_dma_sem("aux", 8)
        q_act = b.new_dma_sem("actq", 8)
        q_pool = b.new_dma_sem("poolq", 2)
        q_out = b.new_dma_sem("outq", 4)

        es0 = es
        iota_col = sb(es0, "iota_col", [128, 1], F32)
        iota_i32 = sb(es0, "iota_i32", [128, 1], I32)
        ones_bf = sb(es0, "ones_bf", [128, 128], BF16)
        ident_f32 = sb(es0, "ident_f32", [128, 128], F32)
        ident_bf = sb(es0, "ident_bf", [128, 128], BF16)
        iof = sb(es0, "iof", [128, 128], F32)
        io32 = sb(es0, "io32", [128, 128], I32)
        bdones = sb(es0, "bdones", [128, 4], BF16)
        repmat = sb(es0, "repmat", [128, D], BF16)
        eps_col = sb(es0, "eps_col", [128, 1], F32)
        kc = {"top": sb(es0, "kc_top", [H, KSEL], F32),
              "bot": sb(es0, "kc_bot", [H, KSEL], F32)}
        idxf = sb(es0, "idxf", [H, NSLOT], F32)
        idxu = sb(es0, "idxu", [H, NSLOT], U32)

        b.op("pool", lambda g: g.iota(iota_i32[:], pattern=[[0, 1]], base=0,
                                      channel_multiplier=1), w=[iota_i32])
        b.op("pool", lambda g: g.tensor_copy(iota_col[:], iota_i32[:]),
             r=[iota_i32], w=[iota_col])
        b.op("pool", lambda g: g.iota(io32[:], pattern=[[1, 128]], base=0,
                                      channel_multiplier=0), w=[io32])
        b.op("pool", lambda g: g.tensor_copy(iof[:], io32[:]), r=[io32], w=[iof])
        b.op("pool", lambda g: g.tensor_scalar(out=ident_f32[:], in0=iof[:],
             scalar1=iota_col[:], scalar2=None, op0=ALU.is_equal),
             r=[iof, iota_col], w=[ident_f32])
        b.op("pool", lambda g: g.memset(ones_bf[:], 1.0), w=[ones_bf])
        b.op("pool", lambda g: g.memset(eps_col[:], 1e-5), w=[eps_col])
        b.op("pool", lambda g: g.tensor_scalar(out=ident_bf[:], in0=iof[:],
             scalar1=iota_col[:], scalar2=None, op0=ALU.is_equal),
             r=[iof, iota_col], w=[ident_bf])
        tmp4a = sb(es0, "tmp4a", [128, 4], F32)
        tmp4b = sb(es0, "tmp4b", [128, 4], F32)
        for j in range(4):
            b.op("pool", lambda g, j=j: g.tensor_scalar(
                out=tmp4a[:, j:j + 1], in0=iota_col[:],
                scalar1=float(j * KSEL) - 0.5, scalar2=None, op0=ALU.is_gt),
                r=[iota_col], w=[tmp4a])
            b.op("pool", lambda g, j=j: g.tensor_scalar(
                out=tmp4b[:, j:j + 1], in0=iota_col[:],
                scalar1=float((j + 1) * KSEL) - 0.5, scalar2=None, op0=ALU.is_lt),
                r=[iota_col], w=[tmp4b])
        b.op("pool", lambda g: g.tensor_tensor(out=bdones[:], in0=tmp4a[:],
             in1=tmp4b[:], op=ALU.mult), r=[tmp4a, tmp4b], w=[bdones])
        for ck in range(DC):
            for hl in range(HPC):
                hglob = HPC * ck + hl
                c0 = ck * 128 + hl * DV
                b.op("pool", lambda g, c0=c0, hglob=hglob: g.tensor_scalar(
                    out=repmat[:, c0:c0 + DV],
                    in0=iota_col[:].to_broadcast([128, 1, DV]).squeeze(1),
                    scalar1=float(hglob), scalar2=None, op0=ALU.is_equal),
                    r=[iota_col], w=[repmat])

        esLN = contextlib.ExitStack()
        es.enter_context(esLN)
        x_ln = sb(esLN, "x_ln", [128, DC * S], BF16)

        NSLOTA = 8
        QROT = (("sp", q_main), ("pool", q_aux), ("act", q_act))
        esA = contextlib.ExitStack()
        with esA:
            xf = sb(esA, "xf", [128, NSLOTA * 2 * TOKB], F32)
            sq_bf = sb(esA, "sq_bf", [128, 4 * 2 * TOKB], BF16)
            mu_rep2 = [sb(esA, "mu_rep0", [128, TOKB], F32),
                       sb(esA, "mu_rep1", [128, TOKB], F32)]
            rstd_rep2 = [sb(esA, "rstd_rep0", [128, TOKB], F32),
                         sb(esA, "rstd_rep1", [128, TOKB], F32)]
            var_t = sb(esA, "var_t0", [128, TOKB], F32)
            mu2_t = sb(esA, "mu2_t", [128, TOKB], F32)
            sum_ps2 = [psum(esA, "sum_ps0", [128, TOKB]),
                       psum(esA, "sum_ps1", [128, TOKB])]
            sq_ps2 = [psum(esA, "sq_ps0", [128, TOKB]),
                      psum(esA, "sq_ps1", [128, TOKB])]

            xT3 = xT.rearrange("(a p) s -> p a s", p=128)
            for tb in range(NSB):
                t0 = tb * TOKB
                mu_rep, rstd_rep = mu_rep2[tb % 2], rstd_rep2[tb % 2]
                sum_ps, sq_ps = sum_ps2[tb % 2], sq_ps2[tb % 2]
                for dcp in range(DC // 2):
                    gi = tb * (DC // 2) + dcp
                    slot = gi % NSLOTA
                    eng, q = QROT[gi % 3]
                    sl2 = xf[:, slot * 2 * TOKB:(slot + 1) * 2 * TOKB]
                    b.dma(eng, q,
                          out=sl2.rearrange("p (a s) -> p a s", a=2),
                          in_=xT3[:, 2 * dcp:2 * dcp + 2, t0:t0 + TOKB],
                          w=[sl2])
                    for j in range(2):
                        dc = 2 * dcp + j
                        src = xf[:, (slot * 2 + j) * TOKB:(slot * 2 + j + 1) * TOKB]
                        sq = sq_bf[:, ((gi % 4) * 2 + j) * TOKB:((gi % 4) * 2 + j + 1) * TOKB]
                        dst = x_ln[:, dc * S + t0: dc * S + t0 + TOKB]
                        b.op("dve", lambda v, src=src, dst=dst: v.tensor_copy(dst, src),
                             r=[src], w=[dst])
                        if j == 0:
                            b.op("pool", lambda g, dst=dst, sq=sq: g.tensor_tensor(
                                out=sq, in0=dst, in1=dst, op=ALU.mult), r=[dst], w=[sq])
                        else:
                            b.op("act", lambda a, dst=dst, sq=sq: a.activation(
                                out=sq, in_=dst, func=AF.Square), r=[dst], w=[sq])
                        for n0, n1 in nspl(TOKB):
                            b.op("pe", lambda t, dst=dst, n0=n0, n1=n1, dc=dc, sum_ps=sum_ps: t.matmul(
                                sum_ps[:, n0:n1], ones_bf[:], dst[:, n0:n1],
                                start=(dc == 0), stop=(dc == DC - 1)),
                                r=[ones_bf, dst], w=[sum_ps])
                            b.op("pe", lambda t, sq=sq, n0=n0, n1=n1, dc=dc, sq_ps=sq_ps: t.matmul(
                                sq_ps[:, n0:n1], ones_bf[:], sq[:, n0:n1],
                                start=(dc == 0), stop=(dc == DC - 1)),
                                r=[ones_bf, sq], w=[sq_ps])
                b.op("dve", lambda v, sum_ps=sum_ps, mu_rep=mu_rep: v.tensor_scalar(
                     out=mu_rep[:], in0=sum_ps[:],
                     scalar1=1.0 / D, scalar2=None, op0=ALU.mult),
                     r=[sum_ps], w=[mu_rep])
                b.op("pool", lambda g, mu_rep=mu_rep: g.tensor_tensor(
                     out=mu2_t[:], in0=mu_rep[:],
                     in1=mu_rep[:], op=ALU.mult), r=[mu_rep], w=[mu2_t])
                b.op("dve", lambda v, sq_ps=sq_ps, var_t=var_t: v.tensor_scalar(
                     out=var_t[:], in0=sq_ps[:],
                     scalar1=1.0 / D, scalar2=None, op0=ALU.mult),
                     r=[sq_ps], w=[var_t])
                b.op("dve", lambda v, var_t=var_t: v.tensor_tensor(
                     out=var_t[:], in0=var_t[:],
                     in1=mu2_t[:], op=ALU.subtract), r=[var_t, mu2_t], w=[var_t])
                b.op("act", lambda a, var_t=var_t, rstd_rep=rstd_rep: a.activation(
                     out=rstd_rep[:], in_=var_t[:],
                     func=AF.Ln, bias=eps_col[:], scale=1.0), r=[var_t, eps_col], w=[rstd_rep])
                b.op("act", lambda a, rstd_rep=rstd_rep: a.activation(
                     out=rstd_rep[:], in_=rstd_rep[:],
                     func=AF.Exp, scale=-0.5), r=[rstd_rep], w=[rstd_rep])
                for dc in range(DC):
                    dst = x_ln[:, dc * S + t0: dc * S + t0 + TOKB]
                    b.op("dve" if dc % 2 else "pool",
                         lambda g, dst=dst, mu_rep=mu_rep: g.tensor_tensor(
                         out=dst, in0=dst, in1=mu_rep[:], op=ALU.subtract),
                         r=[dst, mu_rep], w=[dst])
                    b.op("pool" if dc % 2 else "dve",
                         lambda g, dst=dst, rstd_rep=rstd_rep: g.tensor_tensor(
                        out=dst, in0=dst, in1=rstd_rep[:], op=ALU.mult),
                        r=[dst, rstd_rep], w=[dst])
            b.barrier()
        if "dbg_xln" in dbg:
            t = dbg_out("dbg_xln", [128, DC * S])
            with nc.sbuf_tensor("dxl", [128, DC * S], F32) as dxl:
                b.op("dve", lambda v: v.tensor_copy(dxl[:], x_ln[:]), r=[x_ln], w=["dxl"])
                b.dma("sp", q_out, out=t, in_=dxl[:], r=["dxl"])
                b.barrier()

        esBC = contextlib.ExitStack()
        with esBC:
            q_sb = sb(esBC, "q_sb", [H, NT], F32)
            wk_bf = sb(esBC, "wk_bf", [128, DC * H], BF16)
            wq_bf = sb(esBC, "wq_bf", [128, DC * H], BF16)
            keys_sb = sb(esBC, "keys_sb", [H, S], F32)
            work_a = sb(esBC, "work_a", [H, S], F32)
            work_b = sb(esBC, "work_b", [H, S], F32)
            k_ps = psum(esBC, "k_ps", [128 if H > 128 else H, 512])

            for wdst, wsrc, stg in ((wk_bf, wkT, work_a), (wq_bf, wqT, work_b)):
                fsl = stg[:, 0:DC * H]
                b.dma("sp", q_main, out=fsl.rearrange("p (a h) -> p a h", a=DC),
                      in_=wsrc, w=[fsl])
                b.op("dve", lambda v, fsl=fsl, wdst=wdst: v.tensor_copy(
                    wdst[:], fsl), r=[fsl], w=[wdst])

            for s0 in range(0, S, 512):
                s1 = min(s0 + 512, S)
                for dc in range(DC):
                    b.op("pe", lambda t, dc=dc, s0=s0, s1=s1: t.matmul(
                        k_ps[:, 0:s1 - s0],
                        wk_bf[:, dc * H:(dc + 1) * H],
                        x_ln[:, dc * S + s0: dc * S + s1],
                        start=(dc == 0), stop=(dc == DC - 1)),
                        r=[wk_bf, x_ln], w=[k_ps])
                b.op("act", lambda a, s0=s0, s1=s1: a.copy(
                    keys_sb[:, s0:s1], k_ps[:, 0:s1 - s0]), r=[k_ps], w=[keys_sb])
            for s0, s1 in nspl(NT):
                for dc in range(DC):
                    b.op("pe", lambda t, dc=dc, s0=s0, s1=s1: t.matmul(
                        k_ps[:, 0:s1 - s0],
                        wq_bf[:, dc * H:(dc + 1) * H],
                        x_ln[:, dc * S + s0: dc * S + s1],
                        start=(dc == 0), stop=(dc == DC - 1)),
                        r=[wq_bf, x_ln], w=[k_ps])
                b.op("act", lambda a, s0=s0, s1=s1: a.copy(
                    q_sb[:, s0:s1], k_ps[:, 0:s1 - s0]), r=[k_ps], w=[q_sb])
            b.dma("sp", q_main, out=q_dram, in_=q_sb[:], r=[q_sb])

            for si, (setn, src_sign) in enumerate((("top", 1.0), ("bot", -1.0))):
                if src_sign > 0:
                    b.op("dve", lambda v: v.tensor_copy(work_a[:], keys_sb[:]),
                         r=[keys_sb], w=[work_a])
                else:
                    b.op("dve", lambda v: v.tensor_scalar(out=work_a[:],
                         in0=keys_sb[:], scalar1=-1.0, scalar2=None, op0=ALU.mult),
                         r=[keys_sb], w=[work_a])
                cur, nxt = work_a, work_b
                for r8 in range(KSEL // 8):
                    mslice = kc[setn][:, 8 * r8:8 * r8 + 8]
                    islice = idxu[:, si * KSEL + 8 * r8: si * KSEL + 8 * r8 + 8]
                    b.op("dve", lambda v, m=mslice, c=cur: v.max(out=m, in_=c[:]),
                         r=[cur], w=[kc[setn]])
                    b.op("dve", lambda v, m=mslice, i=islice, c=cur: v.max_index(
                        out=i, in_max=m, in_values=c[:]), r=[kc[setn], cur], w=[idxu])
                    if r8 < KSEL // 8 - 1:
                        b.op("dve", lambda v, m=mslice, c=cur, n=nxt: v.match_replace(
                            out=n[:], in_to_replace=m, in_values=c[:], imm_value=-1e30),
                            r=[kc[setn], cur], w=[nxt])
                        cur, nxt = nxt, cur
                if src_sign < 0:
                    b.op("dve", lambda v: v.tensor_scalar(out=kc["bot"][:],
                         in0=kc["bot"][:], scalar1=-1.0, scalar2=None, op0=ALU.mult),
                         r=[kc["bot"]], w=[kc["bot"]])
            b.op("dve", lambda v: v.tensor_copy(idxf[:], idxu[:]), r=[idxu], w=[idxf])
            b.dma("sp", q_main, out=idxrows_dram.rearrange("(h s) -> h s", h=H),
                  in_=idxf[:], r=[idxf])
            if "dbg_keys" in dbg:
                t = dbg_out("dbg_keys", [H, S])
                b.dma("sp", q_out, out=t, in_=keys_sb[:], r=[keys_sb])
            if "dbg_q" in dbg:
                t = dbg_out("dbg_q", [H, NT])
                b.dma("sp", q_out, out=t, in_=q_sb[:], r=[q_sb])
            if "dbg_kc" in dbg:
                t = dbg_out("dbg_kc", [H, NSLOT])
                with nc.sbuf_tensor("dkc", [H, NSLOT], F32) as dkc:
                    b.op("dve", lambda v: v.tensor_copy(dkc[:, 0:KSEL], kc["top"][:]), r=[kc["top"]], w=["dkc"])
                    b.op("dve", lambda v: v.tensor_copy(dkc[:, KSEL:], kc["bot"][:]), r=[kc["bot"]], w=["dkc"])
                    b.dma("sp", q_out, out=t, in_=dkc[:], r=["dkc"])
                    b.barrier()
            if "dbg_idx" in dbg:
                t = dbg_out("dbg_idx", [H, NSLOT])
                b.dma("sp", q_out, out=t, in_=idxf[:], r=[idxf])
            b.barrier()

        esD = contextlib.ExitStack()
        with esD:
            wv_f = sb(esD, "wv_f", [128, 2 * 4 * FW], F32)
            wv_bf = sb(esD, "wv_bf", [128, DC * FW], BF16)
            v_sb = sb(esD, "v_sb", [128, 2 * FW], BF16)
            p_sb = sb(esD, "p_sb", [128, 2 * UPE * 512], BF16)
            idx_bc = sb(esD, "idx_bc", [128, UPE * 512], F32)
            gev = sb(esD, "gev", [128, 512], BF16)
            vgT = sb(esD, "vgT", [128, G2 * 128], BF16)
            v_ps2 = [psum(esD, "v_ps0", [128, FW]), psum(esD, "v_ps1", [128, FW])]
            g_ps = [psum(esD, f"g_ps{i}", [128, HPU * NSLOT]) for i in range(UPE)]
            t_ps = psum(esD, "t_ps", [128, 512], BF16)

            for ec in range(NEC):
                for dg in range(DC // 4):
                    gi = ec * (DC // 4) + dg
                    eng, q = (("sp", q_main), ("act", q_act))[gi % 2]
                    fsl = wv_f[:, (gi % 2) * 4 * FW:(gi % 2) * 4 * FW + 4 * FW]
                    b.dma(eng, q, out=fsl.rearrange("p (a f) -> p a f", a=4),
                          in_=wvT[ec, :, 4 * dg:4 * dg + 4, :], w=[fsl])
                    for j in range(4):
                        dc = 4 * dg + j
                        b.op("dve", lambda v, fsl=fsl, j=j, dc=dc: v.tensor_copy(
                            wv_bf[:, dc * FW:(dc + 1) * FW],
                            fsl[:, j * FW:(j + 1) * FW]), r=[fsl],
                            w=[wv_bf[:, dc * FW:(dc + 1) * FW]])
                for ul in range(UPE):
                    u = ec * UPE + ul
                    srcap = idxrows_dram[u * HPU * NSLOT:(u + 1) * HPU * NSLOT]
                    b.dma("pool", q_aux,
                          out=idx_bc[:, ul * 512: ul * 512 + HPU * NSLOT],
                          in_=srcap.unsqueeze(0).to_broadcast([128, HPU * NSLOT]),
                          w=[idx_bc])
                for tc in range(NTC):
                    vsl = v_sb[:, (tc % 2) * FW:(tc % 2) * FW + FW]
                    v_ps = v_ps2[tc % 2]
                    for ul in range(UPE):
                        ibc = idx_bc[:, ul * 512: ul * 512 + HPU * NSLOT]
                        psl = p_sb[:, ((tc % 2) * UPE + ul) * 512:
                                   ((tc % 2) * UPE + ul) * 512 + HPU * NSLOT]
                        b.op("dve",
                             lambda v, ibc=ibc, psl=psl, tc=tc: v.tensor_scalar(
                            out=psl, in0=ibc, scalar1=float(tc * 128),
                            scalar2=iota_col[:], op0=ALU.subtract, op1=ALU.is_equal),
                            r=[ibc, iota_col], w=[psl])
                    for dc in range(DC):
                        b.op("pe", lambda t, dc=dc, tc=tc, v_ps=v_ps: t.matmul(
                            v_ps[:],
                            x_ln[:, dc * S + tc * 128: dc * S + (tc + 1) * 128],
                            wv_bf[:, dc * FW:(dc + 1) * FW],
                            start=(dc == 0), stop=(dc == DC - 1)),
                            r=[x_ln[:, dc * S + tc * 128: dc * S + (tc + 1) * 128],
                               wv_bf[:, dc * FW:(dc + 1) * FW]], w=[v_ps])
                    b.op("dve", lambda v, vsl=vsl, v_ps=v_ps: v.tensor_copy(vsl, v_ps[:]),
                         r=[v_ps], w=[vsl])
                    for ul in range(UPE):
                        psl = p_sb[:, ((tc % 2) * UPE + ul) * 512:
                                   ((tc % 2) * UPE + ul) * 512 + HPU * NSLOT]
                        b.op("pe", lambda t, ul=ul, vsl=vsl, psl=psl, tc=tc: t.matmul(
                            g_ps[ul][:], vsl[:, ul * HPU * DV:(ul + 1) * HPU * DV],
                            psl, start=(tc == 0), stop=(tc == NTC - 1)),
                            r=[vsl, psl], w=[g_ps[ul]])
                for ul in range(UPE):
                    u = ec * UPE + ul
                    b.op("dve", lambda v, ul=ul: v.tensor_copy(gev[:, 0:HPU * NSLOT], g_ps[ul][:]),
                         r=[g_ps[ul]], w=[gev])
                    nblk = (HPU * NSLOT) // 128
                    for j in range(nblk):
                        b.op("pe", lambda t, j=j: t.transpose(
                            t_ps[:, j * 128:(j + 1) * 128],
                            gev[0:HPU * DV, j * 128:(j + 1) * 128], ident_bf[:]),
                            r=[gev, ident_bf], w=[t_ps])
                    for j in range(nblk):
                        g2i = u * nblk + j
                        b.op("dve", lambda v, j=j, g2i=g2i: v.tensor_copy(
                            vgT[:, g2i * 128: g2i * 128 + HPU * DV],
                            t_ps[:, j * 128: j * 128 + HPU * DV]),
                            r=[t_ps], w=[vgT])
                ecw = UPE * ((HPU * NSLOT) // 128) * 128
                b.dma("pool", q_out,
                      out=vgt_dram[:, ec * ecw:(ec + 1) * ecw],
                      in_=vgT[:, ec * ecw:(ec + 1) * ecw],
                      r=[vgT])
            if "dbg_vg" in dbg:
                t = dbg_out("dbg_vg", [128, G2 * 128])
                with nc.sbuf_tensor("dvg", [128, G2 * 128], F32) as dvg:
                    b.op("dve", lambda v: v.tensor_copy(dvg[:], vgT[:]), r=[vgT], w=["dvg"])
                    b.dma("sp", q_out, out=t, in_=dvg[:], r=["dvg"])
                    b.barrier()

            b.barrier()
        b.barrier()
        esLN.close()

        esX = contextlib.ExitStack()
        es.enter_context(esX)
        x_ln2 = sb(esX, "x_ln2", [128, DC * NT], BF16)
        c_dense = sb(esX, "c_dense", [128, TSUB * E], F32)

        esEF = contextlib.ExitStack()
        with esEF:
            xg = sb(esEF, "xg", [128, 2 * 128], BF16)
            q_sb = sb(esEF, "q_sb2", [H, NT], F32)
            q_bf = sb(esEF, "q_bf", [H, NT], BF16)
            G_sb = sb(esEF, "G_sb", [H, NT], F32)
            Gc_sb = sb(esEF, "Gc_sb", [H, NT], F32)
            pvl = {"top": sb(esEF, "pvl_top", [128, G4 * 68], BF16),
                   "bot": sb(esEF, "pvl_bot", [128, G4 * 68], BF16)}
            d_all = {"top": sb(esEF, "d_top", [H, NT], F32),
                     "bot": sb(esEF, "d_bot", [H, NT], F32)}
            stage4 = sb(esEF, "stage4", [4, NT], F32)
            e_sb = sb(esEF, "e_sb", [128, 2 * NT], BF16)
            n_one = sb(esEF, "n_one", [128, DC * NT], BF16)
            n_sb = {"top": n_one, "bot": n_one}
            r_bf = {"top": sb(esEF, "r_top", [H, NT], BF16),
                    "bot": sb(esEF, "r_bot", [H, NT], BF16)}
            aout = sb(esEF, "aout", [128, DC * NT], BF16)
            t1f = sb(esEF, "t1f", [128, NT], F32)
            t2f = t1f
            wo_f = sb(esEF, "wo_f", [128, 2 * DC * 128], F32)
            wo_bf = sb(esEF, "wo_bf", [128, 2 * DC * 128], BF16)
            xo_f = sb(esEF, "xo_f", [128, 2 * NT], F32)
            rst = sb(esEF, "rst", [128, NT], F32)
            rc_t = t1f
            esE1 = contextlib.ExitStack()
            l_ps2 = [psum(esE1, "l_ps0", [128, NT]), psum(esE1, "l_ps1", [128, NT])]
            pv_ps = psum(esE1, "pv_ps", [68, NT])
            rp1 = psum(esE1, "rp1", [128, NT])

            b.dma("sp", q_main, out=q_sb[:], in_=q_dram, w=[q_sb])
            b.op("dve", lambda v: v.tensor_copy(q_bf[:], q_sb[:]), r=[q_sb], w=[q_bf])
            b.op("dve", lambda v: v.tensor_scalar(out=G_sb[:], in0=q_sb[:],
                 scalar1=0.0, scalar2=None, op0=ALU.is_gt), r=[q_sb], w=[G_sb])
            b.op("pool", lambda g: g.tensor_scalar(out=Gc_sb[:], in0=G_sb[:],
                 scalar1=1.0, scalar2=-1.0, op0=ALU.subtract, op1=ALU.mult),
                 r=[G_sb], w=[Gc_sb])
            esE0 = contextlib.ExitStack()
            vgT2 = sb(esE0, "vgT2", [128, G2 * 128], BF16)
            b.dma("sp", q_main, out=vgT2[:], in_=vgt_dram, w=[vgT2])
            b.op("pool", lambda g: g.memset(pvl["top"][:], 0.0), w=[pvl["top"]])
            b.op("pool", lambda g: g.memset(pvl["bot"][:], 0.0), w=[pvl["bot"]])
            for gi in range(G4):
                u = (4 * gi) // HPU
                for si, setn in enumerate(("top", "bot")):
                    for j in range(4):
                        h = 4 * gi + j
                        g2i = h // 2
                        par = h % 2
                        kloc = h - u * HPU
                        src_p0 = 64 * par + 32 * si
                        srcp = vgT2[src_p0:src_p0 + KSEL,
                                    g2i * 128 + kloc * DV: g2i * 128 + (kloc + 1) * DV]
                        dstp = pvl[setn][j * KSEL:(j + 1) * KSEL,
                                         gi * 68 + j * DV: gi * 68 + (j + 1) * DV]
                        if (gi + j) % 2 == 0:
                            b.op("act", lambda a, srcp=srcp, dstp=dstp: a.copy(dstp, srcp),
                                 r=[vgT2], w=[pvl[setn]])
                        else:
                            b.op("pool", lambda g, srcp=srcp, dstp=dstp: g.tensor_copy(dstp, srcp),
                                 r=[vgT2], w=[pvl[setn]])
            for gi in range(G4):
                for setn in ("top", "bot"):
                    b.op("pool", lambda g, gi=gi, setn=setn: g.tensor_copy(
                        pvl[setn][:, gi * 68 + 64: gi * 68 + 68], bdones[:]),
                        r=[bdones], w=[pvl[setn][:, gi * 68 + 64: gi * 68 + 68]])
            b.barrier()
            esE0.close()

            def build_x(si, setn, gi):
                xgs = xg[:, (gi % 2) * 128:(gi % 2) * 128 + 128]
                for j in range(4):
                    h = 4 * gi + j
                    b.op("dve", lambda v, xgs=xgs, j=j, h=h, setn=setn: v.tensor_scalar(
                        out=xgs[0:H, j * KSEL:(j + 1) * KSEL], in0=kc[setn][:],
                        scalar1=ident_f32[0:H, h:h + 1], scalar2=None, op0=ALU.mult),
                        r=[kc[setn], ident_f32], w=[xgs])
                return xgs

            xgs_next = build_x(0, "top", 0)
            for si, setn in enumerate(("top", "bot")):
                for gi in range(G4):
                    xgs = xgs_next
                    if gi + 1 < G4:
                        xgs_next = build_x(si, setn, gi + 1)
                    elif si == 0:
                        xgs_next = build_x(1, "bot", 0)
                    esl = e_sb[:, (gi % 2) * NT:(gi % 2) * NT + NT]
                    l_ps = l_ps2[gi % 2]
                    for n0, n1 in nspl(NT):
                        b.op("pe", lambda t, xgs=xgs, n0=n0, n1=n1: t.matmul(
                            l_ps[:, n0:n1], xgs[0:H, :], q_bf[:, n0:n1],
                            start=True, stop=True), r=[xgs, q_bf], w=[l_ps])
                    b.op("act", lambda a, esl=esl: a.activation(
                        out=esl, in_=l_ps[:], func=AF.Exp), r=[l_ps], w=[esl])
                    for n0, n1 in nspl(NT):
                        b.op("pe", lambda t, esl=esl, n0=n0, n1=n1, setn=setn, gi=gi: t.matmul(
                            pv_ps[:, n0:n1], pvl[setn][:, gi * 68:(gi + 1) * 68],
                            esl[:, n0:n1], start=True, stop=True),
                            r=[pvl[setn], esl], w=[pv_ps])
                    b.op("dve", lambda v: v.tensor_copy(stage4[:], pv_ps[64:68, :]),
                         r=[pv_ps], w=[stage4])
                    b.dma("pool", q_aux, out=d_all[setn][4 * gi:4 * gi + 4, :],
                          in_=stage4[:], w=[d_all[setn]])
                    ck, po = gi // 2, 64 * (gi % 2)
                    b.op("dve", lambda v, ck=ck, po=po, setn=setn: v.tensor_copy(
                        n_sb[setn][po:po + 64, ck * NT:(ck + 1) * NT], pv_ps[0:64, :]),
                        r=[pv_ps], w=[n_sb[setn][po:po + 64, ck * NT:(ck + 1) * NT]])
                gsrc = G_sb if setn == "top" else Gc_sb
                b.op("dve", lambda v, setn=setn: v.reciprocal(out=rc_t[0:H, :],
                     in_=d_all[setn][:]), r=[d_all[setn]], w=[rc_t])
                b.op("dve", lambda v, gsrc=gsrc, setn=setn: v.tensor_tensor(
                    out=r_bf[setn][:], in0=rc_t[0:H, :], in1=gsrc[:], op=ALU.mult),
                    r=[rc_t, gsrc], w=[r_bf[setn]])
                for ck in range(DC):
                    for n0, n1 in nspl(NT):
                        b.op("pe", lambda t, ck=ck, setn=setn, n0=n0, n1=n1: t.matmul(
                            rp1[:, n0:n1], repmat[:, ck * 128:(ck + 1) * 128][0:H, :],
                            r_bf[setn][:, n0:n1], start=True, stop=True),
                            r=[repmat, r_bf[setn]], w=[rp1])
                    if setn == "top":
                        b.op("dve", lambda v, ck=ck: v.tensor_tensor(
                            out=aout[:, ck * NT:(ck + 1) * NT], in0=rp1[:],
                            in1=n_one[:, ck * NT:(ck + 1) * NT], op=ALU.mult),
                            r=[rp1, n_one], w=[aout])
                    else:
                        b.op("dve", lambda v, ck=ck: v.tensor_tensor(
                            out=t1f[:], in0=rp1[:],
                            in1=n_one[:, ck * NT:(ck + 1) * NT], op=ALU.mult),
                            r=[rp1, n_one], w=[t1f])
                        b.op("pool", lambda g, ck=ck: g.tensor_tensor(
                            out=aout[:, ck * NT:(ck + 1) * NT],
                            in0=aout[:, ck * NT:(ck + 1) * NT], in1=t1f[:],
                            op=ALU.add), r=[t1f, aout], w=[aout])

            b.barrier()
            esE1.close()
            esE2 = contextlib.ExitStack()
            wo_ps = psum(esE2, "wo_ps", [128, NT])
            if "dbg_aout" in dbg:
                t = dbg_out("dbg_aout", [128, DC * NT])
                with nc.sbuf_tensor("dao", [128, DC * NT], F32) as dao:
                    b.op("dve", lambda v: v.tensor_copy(dao[:], aout[:]), r=[aout], w=["dao"])
                    b.dma("sp", q_out, out=t, in_=dao[:], r=["dao"])
                    b.barrier()

            for oc in range(DC):
                wsl = wo_bf[:, (oc % 2) * DC * 128:(oc % 2) * DC * 128 + DC * 128]
                fsl = wo_f[:, (oc % 2) * DC * 128:(oc % 2) * DC * 128 + DC * 128]
                b.dma("sp", q_main, out=fsl.rearrange("p (a f) -> p a f", a=DC),
                      in_=woT[oc], w=[fsl])
                b.op("dve", lambda v, fsl=fsl, wsl=wsl: v.tensor_copy(wsl, fsl),
                     r=[fsl], w=[wsl])
                xsl = xo_f[:, (oc % 2) * NT:(oc % 2) * NT + NT]
                b.dma("pool", q_aux, out=xsl, in_=xT[oc * 128:(oc + 1) * 128, 0:NT],
                      w=[xsl])
                for n0, n1 in nspl(NT):
                    for dc in range(DC):
                        b.op("pe", lambda t, wsl=wsl, dc=dc, n0=n0, n1=n1: t.matmul(
                            wo_ps[:, n0:n1], wsl[:, dc * 128:(dc + 1) * 128],
                            aout[:, dc * NT + n0: dc * NT + n1],
                            start=(dc == 0), stop=(dc == DC - 1)),
                            r=[wsl[:, dc * 128:(dc + 1) * 128],
                               aout[:, dc * NT + n0: dc * NT + n1]], w=[wo_ps])
                b.op("dve", lambda v, xsl=xsl: v.tensor_tensor(out=rst[:],
                     in0=wo_ps[:], in1=xsl, op=ALU.add), r=[wo_ps, xsl], w=[rst])
                b.dma("sp", q_main, out=resid_dram[oc], in_=rst[:], r=[rst])
            b.barrier()
            esE2.close()
        b.barrier()

        esGH = contextlib.ExitStack()
        es.enter_context(esGH)
        acc = sb(esGH, "acc", [128, TSUB * D], F32)
        esG = contextlib.ExitStack()
        with esG:
            rs_full = sb(esG, "rs_full", [128, DC * NT], F32)
            sq2_bf = sb(esG, "sq2_bf", [128, 2 * NT], BF16)
            mu_rep = sb(esG, "mu_rep2", [128, NT], F32)
            rstd_rep = sb(esG, "rstd_rep2", [128, NT], F32)
            var_t = sb(esG, "var_t2", [128, NT], F32)
            mu2_t = sb(esG, "mu2_t2", [128, NT], F32)
            gw_f = sb(esG, "gw_f", [128, DC * E], F32)
            gw_bf = sb(esG, "gw_bf", [128, DC * E], BF16)
            sc_sb = sb(esG, "sc_sb", [128, E], F32)
            m8_sb = sb(esG, "m8_sb", [128, 8], F32)
            el_sb = sb(esG, "el_sb", [128, E], F32)
            nlx = sb(esG, "nlx", [128, 1], F32)
            den = sb(esG, "den", [128, 1], F32)
            sum_ps = psum(esG, "sum_ps2", [128, NT])
            sq_ps = psum(esG, "sq_ps2", [128, NT])
            sc_ps = psum(esG, "sc_ps", [128, E])
            tr_ps2 = [psum(esG, "tr_ps0", [128, 128]),
                      psum(esG, "tr_ps1", [128, 128])]

            for dc in range(DC):
                rsl = rs_full[:, dc * NT:(dc + 1) * NT]
                eng, q = QROT[dc % 3]
                b.dma(eng, q, out=rsl, in_=resid_dram[dc], w=[rsl])
                xb = x_ln2[:, dc * NT:(dc + 1) * NT]
                sq = sq2_bf[:, (dc % 2) * NT:(dc % 2) * NT + NT]
                b.op("dve", lambda v, rsl=rsl, xb=xb: v.tensor_copy(xb, rsl),
                     r=[rsl], w=[xb])
                b.op("pool", lambda g, xb=xb, sq=sq: g.tensor_tensor(
                    out=sq, in0=xb, in1=xb, op=ALU.mult), r=[xb], w=[sq])
                for n0, n1 in nspl(NT):
                    b.op("pe", lambda t, xb=xb, n0=n0, n1=n1, dc=dc: t.matmul(
                        sum_ps[:, n0:n1], ones_bf[:], xb[:, n0:n1],
                        start=(dc == 0), stop=(dc == DC - 1)),
                        r=[ones_bf, xb], w=[sum_ps])
                    b.op("pe", lambda t, sq=sq, n0=n0, n1=n1, dc=dc: t.matmul(
                        sq_ps[:, n0:n1], ones_bf[:], sq[:, n0:n1],
                        start=(dc == 0), stop=(dc == DC - 1)),
                        r=[ones_bf, sq], w=[sq_ps])
            b.op("dve", lambda v: v.tensor_scalar(out=mu_rep[:], in0=sum_ps[:],
                 scalar1=1.0 / D, scalar2=None, op0=ALU.mult), r=[sum_ps], w=[mu_rep])
            b.op("pool", lambda g: g.tensor_tensor(out=mu2_t[:], in0=mu_rep[:],
                 in1=mu_rep[:], op=ALU.mult), r=[mu_rep], w=[mu2_t])
            b.op("dve", lambda v: v.tensor_scalar(out=var_t[:], in0=sq_ps[:],
                 scalar1=1.0 / D, scalar2=None, op0=ALU.mult), r=[sq_ps], w=[var_t])
            b.op("dve", lambda v: v.tensor_tensor(out=var_t[:], in0=var_t[:],
                 in1=mu2_t[:], op=ALU.subtract), r=[var_t, mu2_t], w=[var_t])
            b.op("act", lambda a: a.activation(out=rstd_rep[:], in_=var_t[:],
                 func=AF.Ln, bias=eps_col[:], scale=1.0), r=[var_t, eps_col], w=[rstd_rep])
            b.op("act", lambda a: a.activation(out=rstd_rep[:], in_=rstd_rep[:],
                 func=AF.Exp, scale=-0.5), r=[rstd_rep], w=[rstd_rep])
            for dc in range(DC):
                xb = x_ln2[:, dc * NT:(dc + 1) * NT]
                b.op("dve" if dc % 2 else "pool",
                     lambda g, xb=xb: g.tensor_tensor(out=xb, in0=xb,
                     in1=mu_rep[:], op=ALU.subtract), r=[xb, mu_rep], w=[xb])
                b.op("pool" if dc % 2 else "dve",
                     lambda g, xb=xb: g.tensor_tensor(
                    out=xb, in0=xb, in1=rstd_rep[:],
                    op=ALU.mult), r=[xb, rstd_rep], w=[xb])
            for ck in range(DC):
                for ts in range(TSUB):
                    tr_ps = tr_ps2[(ck * TSUB + ts) % 2]
                    b.op("pe", lambda t, ck=ck, ts=ts, tr_ps=tr_ps: t.transpose(
                        tr_ps[0:TS128, :],
                        rs_full[:, ck * NT + ts * TS128: ck * NT + (ts + 1) * TS128],
                        ident_f32[:]), r=[rs_full, ident_f32], w=[tr_ps])
                    if ts % 2:
                        b.op("dve", lambda v, ts=ts, ck=ck, tr_ps=tr_ps: v.tensor_copy(
                            acc[0:TS128, ts * D + ck * 128: ts * D + (ck + 1) * 128],
                            tr_ps[0:TS128, :]), r=[tr_ps], w=[acc])
                    else:
                        b.op("act", lambda a, ts=ts, ck=ck, tr_ps=tr_ps: a.copy(
                            acc[0:TS128, ts * D + ck * 128: ts * D + (ck + 1) * 128],
                            tr_ps[0:TS128, :]), r=[tr_ps], w=[acc])
            if "dbg_xln2" in dbg:
                t = dbg_out("dbg_xln2", [128, DC * NT])
                with nc.sbuf_tensor("dx2", [128, DC * NT], F32) as dx2:
                    b.op("dve", lambda v: v.tensor_copy(dx2[:], x_ln2[:]), r=[x_ln2], w=["dx2"])
                    b.dma("sp", q_out, out=t, in_=dx2[:], r=["dx2"])
                    b.barrier()

            b.dma("sp", q_main, out=gw_f[:].rearrange("p (dc e) -> p dc e", dc=DC),
                  in_=gwT, w=[gw_f])
            b.op("dve", lambda v: v.tensor_copy(gw_bf[:], gw_f[:]), r=[gw_f], w=[gw_bf])
            TP = TS128
            for ts in range(TSUB):
                for dc in range(DC):
                    b.op("pe", lambda t, ts=ts, dc=dc: t.matmul(
                        sc_ps[0:TP, :],
                        x_ln2[:, dc * NT + ts * TP: dc * NT + ts * TP + TP],
                        gw_bf[:, dc * E:(dc + 1) * E],
                        start=(dc == 0), stop=(dc == DC - 1)),
                        r=[x_ln2, gw_bf], w=[sc_ps])
                b.op("dve", lambda v: v.tensor_copy(sc_sb[0:TP, :], sc_ps[0:TP, :]),
                     r=[sc_ps], w=[sc_sb])
                b.op("dve", lambda v: v.max(out=m8_sb[0:TP, :], in_=sc_sb[0:TP, :]),
                     r=[sc_sb], w=[m8_sb])
                b.op("dve", lambda v: v.tensor_scalar(out=nlx[0:TP, :], in0=m8_sb[0:TP, 0:1],
                     scalar1=-1.0, scalar2=None, op0=ALU.mult), r=[m8_sb], w=[nlx])
                b.op("act", lambda a: a.activation(out=el_sb[0:TP, :], in_=sc_sb[0:TP, :],
                     func=AF.Exp, bias=nlx[0:TP, :], scale=1.0), r=[sc_sb, nlx], w=[el_sb])
                b.op("dve", lambda v: v.tensor_scalar(out=sc_sb[0:TP, :], in0=sc_sb[0:TP, :],
                     scalar1=m8_sb[0:TP, TOPK - 1:TOPK], scalar2=None, op0=ALU.is_ge),
                     r=[sc_sb, m8_sb], w=[sc_sb])
                b.op("dve", lambda v: v.tensor_tensor(out=el_sb[0:TP, :], in0=el_sb[0:TP, :],
                     in1=sc_sb[0:TP, :], op=ALU.mult), r=[el_sb, sc_sb], w=[el_sb])
                b.op("dve", lambda v: v.reduce_sum(out=den[0:TP, :], in_=el_sb[0:TP, :],
                     axis=AX.X), r=[el_sb], w=[den])
                b.op("dve", lambda v: v.reciprocal(out=den[0:TP, :], in_=den[0:TP, :]),
                     r=[den], w=[den])
                b.op("dve", lambda v, ts=ts: v.tensor_scalar(
                    out=c_dense[0:TP, ts * E:(ts + 1) * E], in0=el_sb[0:TP, :],
                    scalar1=den[0:TP, :], scalar2=None, op0=ALU.mult),
                    r=[el_sb, den], w=[c_dense])
            if "dbg_c" in dbg:
                t = dbg_out("dbg_c", [128, TSUB * E])
                b.dma("sp", q_out, out=t, in_=c_dense[:], r=[c_dense])
            b.barrier()

        esH = contextlib.ExitStack()
        with esH:
            NTH = NT // 2 if NT >= 256 else NT
            h_sb = sb(esH, "h_sb", [128, max(IC * NT, ISC * NTH)], BF16)
            w13g_f = sb(esH, "w13g_f", [128, 2 * DC * 128], F32)
            w13u_f = sb(esH, "w13u_f", [128, 2 * DC * 128], F32)
            w13g_bf = sb(esH, "w13g_bf", [128, 2 * DC * 128], BF16)
            w13u_bf = sb(esH, "w13u_bf", [128, 2 * DC * 128], BF16)
            W2G = 3
            w2_f = sb(esH, "w2_f", [128, 2 * W2G * FW], F32)
            w2_bf = sb(esH, "w2_bf", [128, 2 * W2G * FW], BF16)
            silu_bf = sb(esH, "silu_bf", [128, NT], BF16)
            ps8 = [psum(esH, f"ps8_{i}", [128, 512]) for i in range(8)]
            gg2 = [ps8[0], ps8[1]]
            uu2 = [ps8[2], ps8[3]]

            def w13_slots(icp):
                par2 = icp % 2
                W13W = DC * 128
                return (w13g_bf[:, par2 * W13W: par2 * W13W + W13W],
                        w13u_bf[:, par2 * W13W: par2 * W13W + W13W],
                        w13g_f[:, par2 * W13W: par2 * W13W + W13W],
                        w13u_f[:, par2 * W13W: par2 * W13W + W13W])

            def w13_load(icp, wsrc, gufc_half):
                gslot, uslot, gfs, ufs = w13_slots(icp)
                b.dma("sp", q_main, out=gfs.rearrange("p (dc f) -> p dc f", dc=DC),
                      in_=wsrc[icp], w=[gfs])
                b.op("dve", lambda v, gslot=gslot, gfs=gfs: v.tensor_copy(gslot, gfs),
                     r=[gfs], w=[gslot])
                b.dma("pool", q_aux, out=ufs.rearrange("p (dc f) -> p dc f", dc=DC),
                      in_=wsrc[icp + gufc_half], w=[ufs])
                b.op("act", lambda a, uslot=uslot, ufs=ufs: a.copy(uslot, ufs),
                     r=[ufs], w=[uslot])

            def gu_phase(nexp, gufc_half, wsrc, ntl, first):
                if first:
                    w13_load(0, wsrc, gufc_half)
                for icp in range(gufc_half):
                    gslot, uslot, gfs, ufs = w13_slots(icp)
                    if icp + 1 < gufc_half:
                        w13_load(icp + 1, wsrc, gufc_half)
                    for hh, (n0, n1) in enumerate(nspl(ntl)):
                        par = (icp * len(nspl(ntl)) + hh) % 2
                        gp, up = gg2[par], uu2[par]
                        nw = n1 - n0
                        for dc in range(DC):
                            b.op("pe", lambda t, gslot=gslot, dc=dc, n0=n0, nw=nw, gp=gp: t.matmul(
                                gp[:, 0:nw], gslot[:, dc * 128:(dc + 1) * 128],
                                x_ln2[:, dc * NT + n0: dc * NT + n0 + nw],
                                start=(dc == 0), stop=(dc == DC - 1)),
                                r=[gslot, x_ln2], w=[gp])
                        for dc in range(DC):
                            b.op("pe", lambda t, uslot=uslot, dc=dc, n0=n0, nw=nw, up=up: t.matmul(
                                up[:, 0:nw], uslot[:, dc * 128:(dc + 1) * 128],
                                x_ln2[:, dc * NT + n0: dc * NT + n0 + nw],
                                start=(dc == 0), stop=(dc == DC - 1)),
                                r=[uslot, x_ln2], w=[up])
                        b.op("act", lambda a, n0=n0, nw=nw, gp=gp: a.activation(
                            out=silu_bf[:, n0:n0 + nw], in_=gp[:, 0:nw], func=AF.Silu),
                            r=[gp], w=[silu_bf[:, n0:n0 + nw]])
                        b.op("dve", lambda v, icp=icp, ntl=ntl, n0=n0, nw=nw, up=up: v.tensor_tensor(
                            out=h_sb[:, icp * ntl + n0: icp * ntl + n0 + nw],
                            in0=silu_bf[:, n0:n0 + nw], in1=up[:, 0:nw], op=ALU.mult),
                            r=[silu_bf[:, n0:n0 + nw], up],
                            w=[h_sb[:, icp * ntl + n0: icp * ntl + n0 + nw]])

            def y_phase(ic_n, w2src, ntl, toff, cslot):
                tsn = _ceil(ntl, 128)
                groups = []
                ic0 = 0
                while ic0 < ic_n:
                    g = min(W2G, ic_n - ic0)
                    groups.append((ic0, g))
                    ic0 += g
                def w2_load(fg, gidx):
                    ic0, g = groups[gidx]
                    par = (fg * len(groups) + gidx) % 2
                    wfs = w2_f[:, par * W2G * FW: par * W2G * FW + g * FW]
                    wslot = w2_bf[:, par * W2G * FW: par * W2G * FW + g * FW]
                    b.dma("sp", q_main, out=wfs.rearrange("p (g f) -> p g f", g=g),
                          in_=w2src[fg, :, ic0:ic0 + g, :], w=[wfs])
                    b.op("act", lambda a, wslot=wslot, wfs=wfs: a.copy(
                        wslot, wfs), r=[wfs], w=[wslot])
                    return wslot

                wslot_next = w2_load(0, 0)
                for fg in range(FG):
                    for gidx, (ic0, g) in enumerate(groups):
                        wslot = wslot_next
                        if gidx + 1 < len(groups):
                            wslot_next = w2_load(fg, gidx + 1)
                        elif fg + 1 < FG:
                            wslot_next = w2_load(fg + 1, 0)
                        for gg in range(g):
                            ic = ic0 + gg
                            for ts in range(tsn):
                                tt0 = toff + ts * 128
                                tw = min(128, toff + ntl - tt0)
                                b.op("pe", lambda t, ic=ic, gg=gg, ts=ts, wslot=wslot,
                                     tt0=tt0, tw=tw, ic_n=ic_n, ntl=ntl: t.matmul(
                                    ps8[ts][0:tw, 0:FW],
                                    h_sb[:, ic * ntl + tt0 - toff: ic * ntl + tt0 - toff + tw],
                                    wslot[:, gg * FW:(gg + 1) * FW],
                                    start=(ic == 0), stop=(ic == ic_n - 1)),
                                    r=[h_sb[:, ic * ntl + tt0 - toff: ic * ntl + tt0 - toff + tw],
                                       wslot[:, gg * FW:(gg + 1) * FW]], w=[ps8[ts]])
                    for ts in range(tsn):
                        tt0 = toff + ts * 128
                        tw = min(128, toff + ntl - tt0)
                        tsg = tt0 // 128
                        asl = acc[0:tw, tsg * D + fg * FW: tsg * D + (fg + 1) * FW]
                        if cslot is None:
                            b.op("dve", lambda v, ts=ts, tw=tw, asl=asl: v.tensor_tensor(
                                out=asl, in0=ps8[ts][0:tw, 0:FW], in1=asl,
                                op=ALU.add), r=[ps8[ts], asl], w=[asl])
                        else:
                            b.op("dve", lambda v, ts=ts, tsg=tsg, tw=tw, cslot=cslot, asl=asl: v.scalar_tensor_tensor(
                                out=asl, in0=ps8[ts][0:tw, 0:FW],
                                scalar=c_dense[0:tw, tsg * E + cslot: tsg * E + cslot + 1],
                                in1=asl, op0=ALU.mult, op1=ALU.add),
                                r=[ps8[ts], asl, c_dense], w=[asl])

            for e in range(E):
                gu_phase(e, IC, w13T[e], NT, first=(e == 0))
                if e + 1 < E:
                    w13_load(0, w13T[e + 1], IC)
                y_phase(IC, w2T[e], NT, 0, e)
            for th in range(NT // NTH):
                gu_sh = lambda icp: None
                if th == 0:
                    w13_load(0, sw13T, ISC)
                for icp in range(ISC):
                    gslot, uslot, gfs, ufs = w13_slots(icp)
                    if icp + 1 < ISC:
                        w13_load(icp + 1, sw13T, ISC)
                    elif th + 1 < NT // NTH:
                        w13_load(0, sw13T, ISC)
                    t0 = th * NTH
                    for hh, (n0, n1) in enumerate(nspl(NTH)):
                        par = (icp * len(nspl(NTH)) + hh) % 2
                        gp, up = gg2[par], uu2[par]
                        nw = n1 - n0
                        for dc in range(DC):
                            b.op("pe", lambda t, gslot=gslot, dc=dc, n0=n0, nw=nw, t0=t0, gp=gp: t.matmul(
                                gp[:, 0:nw], gslot[:, dc * 128:(dc + 1) * 128],
                                x_ln2[:, dc * NT + t0 + n0: dc * NT + t0 + n0 + nw],
                                start=(dc == 0), stop=(dc == DC - 1)),
                                r=[gslot, x_ln2], w=[gp])
                        for dc in range(DC):
                            b.op("pe", lambda t, uslot=uslot, dc=dc, n0=n0, nw=nw, t0=t0, up=up: t.matmul(
                                up[:, 0:nw], uslot[:, dc * 128:(dc + 1) * 128],
                                x_ln2[:, dc * NT + t0 + n0: dc * NT + t0 + n0 + nw],
                                start=(dc == 0), stop=(dc == DC - 1)),
                                r=[uslot, x_ln2], w=[up])
                        b.op("act", lambda a, n0=n0, nw=nw, gp=gp: a.activation(
                            out=silu_bf[:, n0:n0 + nw], in_=gp[:, 0:nw], func=AF.Silu),
                            r=[gp], w=[silu_bf[:, n0:n0 + nw]])
                        b.op("dve", lambda v, icp=icp, n0=n0, nw=nw, up=up: v.tensor_tensor(
                            out=h_sb[:, icp * NTH + n0: icp * NTH + n0 + nw],
                            in0=silu_bf[:, n0:n0 + nw], in1=up[:, 0:nw], op=ALU.mult),
                            r=[silu_bf[:, n0:n0 + nw], up],
                            w=[h_sb[:, icp * NTH + n0: icp * NTH + n0 + nw]])
                y_phase(ISC, sw2T, NTH, th * NTH, None)

            for ts in range(TSUB):
                b.dma("sp", q_out,
                      out=out_hbm[ts * TS128:(ts + 1) * TS128, :],
                      in_=acc[0:TS128, ts * D:(ts + 1) * D], r=[acc])
            b.final_wait("sp", q_out)
            if dbg:
                b.barrier()
        b.flush()
    return nc, DBG



def _prep_weights(cfg, Wq, Wk, Wv, Wo, gw, w13, w2, sw13, sw2):
    D, E, I, ISH = cfg["D"], cfg["E"], cfg["I"], cfg["ISH"]
    DC = D // 128
    FW = min(512, D)
    NEC = _ceil(D, 512)
    FG = _ceil(D, FW)
    IC = I // 128
    ISC = ISH // 128
    GUFC = (2 * I) // 128
    GUFSC = (2 * ISH) // 128
    H = D // cfg["DV"]
    c = np.ascontiguousarray
    out = {}
    out["wqT"] = c(Wq.T.reshape(DC, 128, H).transpose(1, 0, 2))
    out["wkT"] = c(Wk.T.reshape(DC, 128, H).transpose(1, 0, 2))
    out["wvT"] = c(Wv.T.reshape(DC, 128, NEC, FW).transpose(2, 1, 0, 3))
    out["woT"] = c(Wo.T.reshape(DC, 128, DC, 128).transpose(2, 1, 0, 3))
    out["gwT"] = c(gw.T.reshape(DC, 128, E).transpose(1, 0, 2))
    out["w13T"] = c(w13.transpose(0, 2, 1).reshape(E, DC, 128, GUFC, 128)
                    .transpose(0, 3, 2, 1, 4))
    out["w2T"] = c(w2.transpose(0, 2, 1).reshape(E, IC, 128, FG, FW)
                   .transpose(0, 3, 2, 1, 4))
    out["sw13T"] = c(sw13.T.reshape(DC, 128, GUFSC, 128).transpose(2, 1, 0, 3))
    out["sw2T"] = c(sw2.T.reshape(ISC, 128, FG, FW).transpose(2, 1, 0, 3))
    return out


def host_inmaps(cfg, x, Wq, Wk, Wv, Wo, gw, w13, w2, sw13, sw2):
    D, S, TOK = cfg["D"], cfg["S"], cfg["TOK"]
    CPB = S // TOK
    w = _prep_weights(cfg, Wq, Wk, Wv, Wo, gw, w13, w2, sw13, sw2)
    in_maps = []
    for core in range(CORES):
        bb, q = core // CPB, core % CPB
        xb = x[bb].T
        perm = np.concatenate([np.arange(q * TOK, (q + 1) * TOK),
                               np.arange(0, q * TOK),
                               np.arange((q + 1) * TOK, S)])
        in_maps.append(dict(xT=np.ascontiguousarray(xb[:, perm]), **w))
    return in_maps


def assemble_out(cfg, results, B):
    D, S, TOK = cfg["D"], cfg["S"], cfg["TOK"]
    CPB = S // TOK
    out = np.empty((B, S, D), np.float32)
    for core in range(CORES):
        bb, q = core // CPB, core % CPB
        out[bb, q * TOK:(q + 1) * TOK, :] = results[core]["out"]
    return out


def kernel(L0_attn_x, L0_attn_Wq, L0_attn_Wk, L0_attn_Wv, L0_attn_Wo,
           L0_ffn_gate_w, L0_ffn_w13, L0_ffn_w2, L0_ffn_sw13, L0_ffn_sw2):
    cfg = CFG_FULL
    x = np.ascontiguousarray(np.asarray(L0_attn_x, dtype=np.float32))
    args = [np.asarray(a, dtype=np.float32) for a in (
        L0_attn_Wq, L0_attn_Wk, L0_attn_Wv, L0_attn_Wo, L0_ffn_gate_w,
        L0_ffn_w13, L0_ffn_w2, L0_ffn_sw13, L0_ffn_sw2)]
    nc, _ = build_program(cfg)
    in_maps = host_inmaps(cfg, x, *args)
    res = run_bass_kernel_spmd(nc, in_maps, core_ids=list(range(CORES)))
    return assemble_out(cfg, res.results, x.shape[0])
